# version 11
# baseline (speedup 1.0000x reference)
import math
import numpy as np
from contextlib import ExitStack

import concourse.bass as bass
import concourse.mybir as mybir
from concourse.bass_utils import run_bass_kernel_spmd

F32 = mybir.dt.float32
BF16 = mybir.dt.bfloat16
I32 = mybir.dt.int32
AF = mybir.ActivationFunctionType
ALU = mybir.AluOpType

NC = 8
P = 128
T = 1024
D = 2048
S = 8192
DT = D // P
EPS = 1e-5
SC0 = 1.0 / math.sqrt(32.0)
SC1 = 1.0 / math.sqrt(128.0)
C_UG = 1.0 / math.sqrt(2048.0)
C_DP = 1.0 / math.sqrt(8192.0)
HT0 = 64
HT1 = 44

ENGINES = ("sync", "tensor", "scalar", "vector", "gpsimd")


class Prog:

    def __init__(self, nc):
        self.nc = nc
        self.steps = {e: [] for e in ENGINES}
        self.sems = {}
        self.counts = {}

    def sem(self, name, stack):
        s = stack.enter_context(self.nc.semaphore(name))
        self.sems[name] = s
        self.counts[name] = 0

    def wait(self, eng, sname, target=None):
        t = self.counts[sname] if target is None else target
        if t <= 0:
            return
        s = self.sems[sname]
        self.steps[eng].append(lambda e, s=s, t=t: e.wait_ge(s, t))

    def run(self, eng, fn, inc=None, dma=False):
        amt = 16 if dma else 1
        if inc is not None:
            self.counts[inc] += amt
            s = self.sems[inc]

            def step(e, fn=fn, s=s, amt=amt):
                fn(e).then_inc(s, amt)

            self.steps[eng].append(step)
        else:
            self.steps[eng].append(lambda e, fn=fn: fn(e))

    def cnt(self, sname):
        return self.counts[sname]

    SELF_SEM = {"vector": "s_dve", "scalar": "s_act", "tensor": "s_pe"}

    def selfsync(self, eng):
        self.wait(eng, self.SELF_SEM[eng])

    def emit(self, stack):
        block = stack.enter_context(self.nc.Block())
        for eng in ENGINES:
            deco = getattr(block, eng)
            steps = self.steps[eng]
            if not steps:
                continue

            def body(e, steps=steps):
                for st_ in steps:
                    st_(e)

            deco(body)


def emit_ln(p, r_tiles, xn_buf, xbf_buf, ps, rows, ones_col, ones_row,
            su_ptr=None, dst32=None):
    p.selfsync("vector")
    for t in range(DT):
        p.run("vector", lambda e, t=t: e.tensor_copy(
            out=xbf_buf[:, t*T:(t+1)*T], in_=r_tiles[t]), inc="s_dve")
    p.wait("tensor", "s_dve")
    for half in range(2):
        for t in range(DT):
            p.run("tensor", lambda e, t=t, half=half: e.matmul(
                ps[0:1, half*512:(half+1)*512], ones_col[:, 0:1],
                xbf_buf[:, t*T + half*512: t*T + (half+1)*512],
                start=(t == 0), stop=(t == DT-1)),
                inc="s_pe" if t == DT-1 else None)
    sums_done = p.cnt("s_pe")
    p.wait("scalar", "s_pe", sums_done)
    for t in range(DT):
        p.run("scalar", lambda e, t=t: e.activation(
            out=xbf_buf[:, t*T:(t+1)*T], in_=xbf_buf[:, t*T:(t+1)*T],
            func=AF.Square), inc="s_act")
    p.wait("tensor", "s_act")
    for half in range(2):
        for t in range(DT):
            p.run("tensor", lambda e, t=t, half=half: e.matmul(
                ps[0:1, 1024 + half*512: 1024 + (half+1)*512], ones_col[:, 0:1],
                xbf_buf[:, t*T + half*512: t*T + (half+1)*512],
                start=(t == 0), stop=(t == DT-1)),
                inc="s_pe" if t == DT-1 else None)
    p.wait("vector", "s_pe")
    p.run("vector", lambda e: e.tensor_scalar(
        out=rows[0:1, 0:1024], in0=ps[0:1, 0:1024], scalar1=1.0/D,
        scalar2=None, op0=ALU.mult), inc="s_dve")
    p.run("vector", lambda e: e.tensor_scalar(
        out=rows[0:1, 1024:2048], in0=ps[0:1, 1024:2048], scalar1=1.0/D,
        scalar2=None, op0=ALU.mult), inc="s_dve")
    p.wait("scalar", "s_dve")
    p.run("scalar", lambda e: e.activation(
        out=rows[0:1, 2048:3072], in_=rows[0:1, 0:1024], func=AF.Square),
        inc="s_act")
    p.wait("vector", "s_act")
    p.selfsync("vector")
    p.run("vector", lambda e: e.scalar_tensor_tensor(
        out=rows[0:1, 1024:2048], in0=rows[0:1, 1024:2048], scalar=EPS,
        in1=rows[0:1, 2048:3072], op0=ALU.add, op1=ALU.subtract), inc="s_dve")
    p.wait("scalar", "s_dve")
    p.run("scalar", lambda e: e.activation(
        out=rows[0:1, 2048:3072], in_=rows[0:1, 1024:2048], func=AF.Ln),
        inc="s_act")
    p.selfsync("scalar")
    p.run("scalar", lambda e: e.activation(
        out=rows[0:1, 2048:3072], in_=rows[0:1, 2048:3072], func=AF.Exp,
        scale=-0.5), inc="s_act")
    p.wait("vector", "s_act")
    p.selfsync("vector")
    p.run("vector", lambda e: e.tensor_tensor(
        out=rows[0:1, 0:1024], in0=rows[0:1, 0:1024],
        in1=rows[0:1, 2048:3072], op=ALU.mult), inc="s_dve")
    p.wait("tensor", "s_dve")
    for half in range(2):
        p.run("tensor", lambda e, half=half: e.matmul(
            ps[:, half*512:(half+1)*512], ones_row[0:1, :],
            rows[0:1, 2048 + half*512: 2048 + (half+1)*512],
            start=True, stop=True), inc="s_pe")
        p.run("tensor", lambda e, half=half: e.matmul(
            ps[:, 1024 + half*512: 1024 + (half+1)*512], ones_row[0:1, :],
            rows[0:1, half*512: (half+1)*512],
            start=True, stop=True), inc="s_pe")
    p.wait("vector", "s_pe")
    for t in range(DT):
        for half in range(2):
            sl = slice(t*T + half*512, t*T + (half+1)*512)
            hs = slice(half*512, (half+1)*512)
            dst = dst32 if dst32 is not None else xn_buf
            p.run("vector", lambda e, t=t, sl=sl, hs=hs, dst=dst: e.tensor_tensor(
                out=dst[:, sl], in0=r_tiles[t][:, hs],
                in1=ps[:, hs], op=ALU.mult), inc="s_dve")
            p.selfsync("vector")
            p.run("vector", lambda e, sl=sl, hs=hs, dst=dst: e.tensor_tensor(
                out=dst[:, sl], in0=dst[:, sl],
                in1=ps[:, slice(1024 + hs.start, 1024 + hs.stop)],
                op=ALU.subtract), inc="s_dve")
    r_done = p.cnt("s_dve")
    if su_ptr is not None:
        p.selfsync("vector")
        dst = dst32 if dst32 is not None else xn_buf
        for t in range(DT):
            p.run("vector", lambda e, t=t, dst=dst: e.tensor_scalar(
                out=dst[:, t*T:(t+1)*T], in0=dst[:, t*T:(t+1)*T],
                scalar1=su_ptr[:, t:t+1], scalar2=None, op0=ALU.mult),
                inc="s_dve")
    return r_done


def build_launch_a():
    nc = bass.Bass("TRN2", target_bir_lowering=False)
    xT = nc.dram_tensor("xT", [D, T], F32, kind="ExternalInput").ap()
    Wq_s = nc.dram_tensor("Wq_s", [DT, P, DT*P], F32, kind="ExternalInput").ap()
    Wk_s = nc.dram_tensor("Wk_s", [DT, P, DT*P], F32, kind="ExternalInput").ap()
    Wv_r = nc.dram_tensor("Wv_r", [DT, P, D], F32, kind="ExternalInput").ap()
    bq_i = nc.dram_tensor("bq", [P, DT], F32, kind="ExternalInput").ap()
    bk_i = nc.dram_tensor("bk", [P, DT], F32, kind="ExternalInput").ap()
    qT_o = nc.dram_tensor("qT_o", [D, T], BF16, kind="ExternalOutput").ap()
    kT_o = nc.dram_tensor("kT_o", [D, T], BF16, kind="ExternalOutput").ap()
    v_o = nc.dram_tensor("v_o", [T, D], BF16, kind="ExternalOutput").ap()

    with ExitStack() as st:
        E = st.enter_context
        p = Prog(nc)
        for sname in ("s_x", "s_w0", "s_w1", "s_wv", "s_pe", "s_act", "s_dve",
                      "s_out", "s_so0", "s_so1", "s_vo0", "s_vo1"):
            p.sem(sname, st)

        xn = E(nc.sbuf_tensor("xn", [P, DT*T], BF16))
        xbf = E(nc.sbuf_tensor("xbf", [P, DT*T], BF16))
        rows = E(nc.sbuf_tensor("rows", [1, 3072], F32))
        ones_col = E(nc.sbuf_tensor("ones_col", [P, 1], BF16))
        ones_row = E(nc.sbuf_tensor("ones_row", [1, P], F32))
        bq = E(nc.sbuf_tensor("bq_s", [P, DT], F32))
        bk = E(nc.sbuf_tensor("bk_s", [P, DT], F32))
        wbuf = E(nc.sbuf_tensor("wbuf", [P, 2*DT*P], BF16))
        stage = E(nc.sbuf_tensor("stage", [P, 2*512], BF16))
        vstage = E(nc.sbuf_tensor("vstage", [P, 2*D], BF16))
        ps = E(nc.psum_tensor("ps8", [P, 4096], F32))

        r_ctx = nc.sbuf_tensor("r", [P, DT*T], F32)
        r = r_ctx.__enter__()
        r_tiles = [r[:, t*T:(t+1)*T] for t in range(DT)]

        p.run("vector", lambda e: e.memset(ones_col[:], 1.0), inc="s_dve")
        p.run("vector", lambda e: e.memset(ones_row[:], 1.0), inc="s_dve")
        for t in range(DT):
            p.run("sync", lambda e, t=t: e.dma_start(
                out=r_tiles[t], in_=xT[t*P:(t+1)*P, :]), inc="s_x", dma=True)
        p.run("gpsimd", lambda e: e.dma_start(out=bq[:], in_=bq_i[:]),
              inc="s_x", dma=True)
        p.run("gpsimd", lambda e: e.dma_start(out=bk[:], in_=bk_i[:]),
              inc="s_x", dma=True)
        for eng in ("vector", "scalar", "tensor"):
            p.wait(eng, "s_x")

        r_done = emit_ln(p, r_tiles, xn, xbf, ps, rows, ones_col, ones_row)
        xn_ready = p.cnt("s_dve")
        p.wait("tensor", "s_dve", xn_ready)

        strip_plan = [("q", e_) for e_ in range(DT)] + [("k", e_) for e_ in range(DT)]
        strip_done_dve = {}
        stage_out_cnt = {}
        group_evac_dve = {}
        for si, (kind, E_) in enumerate(strip_plan):
            slot = si % 2
            wsem = "s_w%d" % slot
            src = Wq_s if kind == "q" else Wk_s
            bias = bq if kind == "q" else bk
            dst = qT_o if kind == "q" else kT_o
            if si >= 2:
                p.wait("gpsimd", "s_dve", strip_done_dve[si - 2])
            p.run("gpsimd", lambda e, src=src, E_=E_, slot=slot: e.dma_start(
                out=wbuf[:, slot*DT*P:(slot+1)*DT*P], in_=src[E_]),
                inc=wsem, dma=True)
            p.wait("tensor", wsem, p.cnt(wsem))
            for ch in range(2):
                g = si * 2 + ch
                pband = 4 + (g % 2)
                poff = pband * 512
                if g >= 2:
                    p.wait("tensor", "s_dve", group_evac_dve[g - 2])
                for t in range(DT):
                    p.run("tensor", lambda e, slot=slot, t=t, ch=ch, poff=poff: e.matmul(
                        ps[:, poff:poff+512],
                        wbuf[:, slot*DT*P + t*P: slot*DT*P + (t+1)*P],
                        xn[:, t*T + ch*512: t*T + (ch+1)*512],
                        start=(t == 0), stop=(t == DT-1)),
                        inc="s_pe" if t == DT-1 else None)
                p.wait("vector", "s_pe")
                sslot = g % 2
                osem = "s_so%d" % sslot
                if g >= 2:
                    p.wait("vector", osem, stage_out_cnt[g - 2])
                p.run("vector", lambda e, poff=poff, sslot=sslot, bias=bias, E_=E_: e.tensor_scalar(
                    out=stage[:, sslot*512:(sslot+1)*512],
                    in0=ps[:, poff:poff+512],
                    scalar1=bias[:, E_:E_+1], scalar2=None, op0=ALU.add),
                    inc="s_dve")
                group_evac_dve[g] = p.cnt("s_dve")
                p.wait("sync", "s_dve", group_evac_dve[g])
                p.run("sync", lambda e, dst=dst, E_=E_, ch=ch, sslot=sslot: e.dma_start(
                    out=dst[E_*P:(E_+1)*P, ch*512:(ch+1)*512],
                    in_=stage[:, sslot*512:(sslot+1)*512]),
                    inc=osem, dma=True)
                stage_out_cnt[g] = p.cnt(osem)
            strip_done_dve[si] = p.cnt("s_dve")

        r_ctx.__exit__(None, None, None)
        wv_ctx = nc.sbuf_tensor("wv", [P, DT*D], BF16)
        wv = wv_ctx.__enter__()

        p.wait("gpsimd", "s_dve", r_done)
        for t in range(DT):
            p.run("gpsimd", lambda e, t=t: e.dma_start(
                out=wv[:, t*D:(t+1)*D], in_=Wv_r[t]), inc="s_wv", dma=True)
        p.wait("tensor", "s_wv", p.cnt("s_wv"))
        v_evac_dve = {}
        v_out_cnt = {}
        for tt in range(8):
            vslot = tt % 2
            if tt >= 1:
                p.wait("tensor", "s_dve", v_evac_dve[tt - 1])
            for ec in range(4):
                poff = ec * 512
                for t in range(DT):
                    p.run("tensor", lambda e, t=t, tt=tt, ec=ec, poff=poff: e.matmul(
                        ps[:, poff:poff+512],
                        xn[:, t*T + tt*128: t*T + tt*128 + 128],
                        wv[:, t*D + ec*512: t*D + (ec+1)*512],
                        start=(t == 0), stop=(t == DT-1)),
                        inc="s_pe" if t == DT-1 else None)
            p.wait("vector", "s_pe")
            vsem = "s_vo%d" % vslot
            if tt >= 2:
                p.wait("vector", vsem, v_out_cnt[tt - 2])
            for ec in range(4):
                p.run("vector", lambda e, vslot=vslot, ec=ec: e.tensor_copy(
                    out=vstage[:, vslot*D + ec*512: vslot*D + (ec+1)*512],
                    in_=ps[:, ec*512:(ec+1)*512]), inc="s_dve")
            v_evac_dve[tt] = p.cnt("s_dve")
            p.wait("sync", "s_dve", v_evac_dve[tt])
            p.run("sync", lambda e, tt=tt, vslot=vslot: e.dma_start(
                out=v_o[tt*128:(tt+1)*128, :],
                in_=vstage[:, vslot*D:(vslot+1)*D]), inc=vsem, dma=True)
            v_out_cnt[tt] = p.cnt(vsem)

        for sname in ("s_so0", "s_so1", "s_vo0", "s_vo1"):
            p.wait("gpsimd", sname, p.cnt(sname))
        p.emit(st)
        wv_ctx.__exit__(None, None, None)
    return nc


class Mem:

    def __init__(self, nc, stack, kib, name="arena"):
        self.t = stack.enter_context(
            nc.sbuf_tensor(name, [P, kib * 1024], mybir.dt.uint8))
        self.cap = kib * 1024

    def at(self, byte_off, cols, dtype):
        nb = cols * mybir.dt.size(dtype)
        assert byte_off % 512 == 0 and byte_off + nb <= self.cap, \
            f"bad arena window {byte_off}+{nb} cap {self.cap}"
        return self.t[:, byte_off:byte_off + nb].bitcast(dtype)


def build_launch_b(upto=9, dump=None, p5_tiles=64):
    nc = bass.Bass("TRN2", target_bir_lowering=False)
    dram = lambda n, s, d, k="ExternalInput": nc.dram_tensor(n, s, d, kind=k).ap()
    xT = dram("xT", [D, T], F32)
    qT_i = dram("qT_i", [D, T], BF16)
    kTp = dram("kTp", [16, P, S], BF16)
    vaug = dram("vaug", [16, P, 64*130], BF16)
    Wo_s = dram("Wo_s", [DT, P, DT*P], F32)
    bo_i = dram("bo", [P, DT], F32)
    bv_i = dram("bv", [P, DT], F32)
    suug_i = dram("suug", [P, DT], F32)
    sudp_i = dram("sudp", [P, 64], F32)
    wsu_i = dram("wsu", [P, 64], F32)
    wsg_i = dram("wsg", [P, 64], F32)
    wsd_i = dram("wsd", [P, DT], F32)
    H128_i = dram("H128", [P, P], BF16)
    id64_i = dram("id64", [64, 64], BF16)
    WupT_s = dram("WupT_s", [64, P, DT*P], F32)
    WgT_s = dram("WgT_s", [64, P, DT*P], F32)
    WdT_s = dram("WdT_s", [DT, P, 64*P], F32)
    wq1_s = dram("wq1_s", [DT, P, DT*P], F32)
    wk1_s = dram("wk1_s", [DT, P, DT*P], F32)
    wv1_r = dram("wv1_r", [DT, P, D], F32)
    r2_o = dram("r2_o", [D, T], F32, "ExternalOutput")
    q1T_o = dram("q1T_o", [D, T], BF16, "ExternalOutput")
    k1T_o = dram("k1T_o", [D, T], BF16, "ExternalOutput")
    v1_o = dram("v1_o", [T, D], BF16, "ExternalOutput")
    r1_d = nc.dram_tensor("r1_d", [D, T], F32, kind="Internal").ap()
    dbg_o = nc.dram_tensor("dbg_o", [64*P, T], F32, kind="ExternalOutput").ap() \
        if dump else None

    NBATCH = [3]*21 + [1]
    K = 1024

    with ExitStack() as st:
        E = st.enter_context
        p = Prog(nc)
        for sname in ("s_ld", "s_kv0", "s_kv1", "s_w0", "s_w1", "s_w2", "s_w3",
                      "s_wv", "s_rd0", "s_rd1", "s_pe", "s_act", "s_dve",
                      "s_st0", "s_st1"):
            p.sem(sname, st)

        rows = E(nc.sbuf_tensor("rows", [1, 3072], F32))
        ones_col = E(nc.sbuf_tensor("ones_col", [P, 1], BF16))
        ones_row = E(nc.sbuf_tensor("ones_row", [1, P], F32))
        ones_blk = E(nc.sbuf_tensor("ones_blk", [P, 64], F32))
        id64 = E(nc.sbuf_tensor("id64_s", [64, 64], BF16))
        H128 = E(nc.sbuf_tensor("H128s", [P, P], BF16))
        bo = E(nc.sbuf_tensor("bo_s", [P, DT], F32))
        bv = E(nc.sbuf_tensor("bv_s", [P, DT], F32))
        bvb = E(nc.sbuf_tensor("bvb_s", [P, DT], BF16))
        cb = E(nc.sbuf_tensor("cb_s", [P, DT], F32))
        suug = E(nc.sbuf_tensor("suug_s", [P, DT], F32))
        sudp = E(nc.sbuf_tensor("sudp_s", [P, 64], F32))
        usc = E(nc.sbuf_tensor("usc_s", [P, 64], F32))
        gsc = E(nc.sbuf_tensor("gsc_s", [P, 64], F32))
        dsc = E(nc.sbuf_tensor("dsc_s", [P, DT], F32))
        ps = E(nc.psum_tensor("ps8", [P, 4096], F32))
        M = Mem(nc, st, 184)

        def ld(dst, src):
            p.run("sync", lambda e, dst=dst, src=src: e.dma_start(
                out=dst, in_=src), inc="s_ld", dma=True)
        ld(bo[:], bo_i)
        ld(bv[:], bv_i)
        ld(suug[:], suug_i)
        ld(sudp[:], sudp_i)
        ld(usc[:], wsu_i)
        ld(gsc[:], wsg_i)
        ld(dsc[:], wsd_i)
        ld(H128[:], H128_i)
        ld(id64[:], id64_i)
        p.run("vector", lambda e: e.memset(ones_col[:], 1.0), inc="s_dve")
        p.run("vector", lambda e: e.memset(ones_row[:], 1.0), inc="s_dve")
        p.run("vector", lambda e: e.memset(ones_blk[:], 1.0), inc="s_dve")
        p.wait("vector", "s_ld", p.cnt("s_ld"))
        p.run("vector", lambda e: e.tensor_scalar(
            out=usc[:], in0=usc[:], scalar1=C_UG, scalar2=None, op0=ALU.mult),
            inc="s_dve")
        p.selfsync("vector")
        p.run("vector", lambda e: e.tensor_tensor(
            out=usc[:], in0=usc[:], in1=sudp[:], op=ALU.mult), inc="s_dve")
        p.run("vector", lambda e: e.tensor_scalar(
            out=gsc[:], in0=gsc[:], scalar1=C_UG, scalar2=None, op0=ALU.mult),
            inc="s_dve")
        p.run("vector", lambda e: e.tensor_scalar(
            out=dsc[:], in0=dsc[:], scalar1=C_DP, scalar2=None, op0=ALU.mult),
            inc="s_dve")
        p.run("vector", lambda e: e.tensor_copy(out=bvb[:], in_=bv[:]),
              inc="s_dve")
        const_done = p.cnt("s_dve")

        attnT = M.at(0, DT*T, BF16)
        kbuf = M.at(32*K, 2*S, BF16)
        vbuf = M.at(64*K, 2*64*130, BF16)
        ptb = M.at(98*K, 2*2*3*512, BF16)
        qbuf = M.at(110*K, 2*1024, BF16)
        tmp = M.at(114*K, 1024, BF16)
        mvin = M.at(116*K, 512, BF16)
        rec = M.at(118*K, 1024, F32)
        silu = M.at(162*K, 4*512, BF16)
        rch = M.at(176*K, 2*512, F32)
        stg = M.at(180*K, 2*512, F32)
        vstage = M.at(104*K, 2*D, BF16)
        stgb = M.at(112*K, 2*512, BF16)
        r1 = M.at(32*K, DT*T, F32)
        wstrip = M.at(96*K, 2*DT*P, BF16)
        xbf1 = M.at(96*K, DT*T, BF16)
        xn1f = M.at(104*K, DT*T, F32)
        x1 = M.at(0, DT*T, BF16)
        fscr = M.at(96*K, T, F32)
        hid = M.at(32*K, 64*T, BF16)
        wug = M.at(168*K, 4*DT*P, BF16)
        hscr = M.at(160*K, T, BF16)
        dbuf = M.at(0, 2*64*P, BF16)
        xn2 = M.at(0, DT*T, BF16)
        r2 = M.at(32*K, DT*T, F32)
        xbf2 = M.at(96*K, DT*T, BF16)
        wv1 = M.at(32*K, DT*D, BF16)
        wqk = M.at(96*K, 2*DT*P, BF16)

        def dump_tiles(tiles):
            p.wait("gpsimd", "s_dve", p.cnt("s_dve"))
            p.wait("gpsimd", "s_act", p.cnt("s_act"))
            p.wait("gpsimd", "s_pe", p.cnt("s_pe"))
            for i, ap in enumerate(tiles):
                p.run("gpsimd", lambda e, i=i, ap=ap: e.dma_start(
                    out=dbg_o[i*P:(i+1)*P, :], in_=ap), inc="s_st0", dma=True)

        def finish():
            for sname in ("s_st0", "s_st1"):
                p.wait("gpsimd", sname, p.cnt(sname))
            p.emit(st)

        p.wait("tensor", "s_dve", const_done)
        kv_used_pe = {}
        q_used_pe = {}
        norm_done_dve = 0
        for hp in range(16):
            slot = hp % 2
            kvsem = "s_kv%d" % slot
            if hp >= 2:
                p.wait("sync", "s_pe", kv_used_pe[hp - 2])
            p.run("sync", lambda e, hp=hp, slot=slot: e.dma_start(
                out=kbuf[:, slot*S:(slot+1)*S], in_=kTp[hp]), inc=kvsem, dma=True)
            p.run("sync", lambda e, hp=hp, slot=slot: e.dma_start(
                out=vbuf[:, slot*64*130:(slot+1)*64*130], in_=vaug[hp]),
                inc=kvsem, dma=True)
            p.run("sync", lambda e, hp=hp, slot=slot: e.dma_start(
                out=qbuf[:, slot*1024:(slot+1)*1024], in_=qT_i[hp*P:(hp+1)*P, :]),
                inc=kvsem, dma=True)
            p.wait("tensor", kvsem, p.cnt(kvsem))
            for ch in range(2):
                qsl = slice(slot*1024 + ch*512, slot*1024 + (ch+1)*512)
                asl = slice(hp*T + ch*512, hp*T + (ch+1)*512)
                exp_done = {}
                av_done = {}
                batch_of = []
                kt0 = 0
                for bs in NBATCH:
                    batch_of.append((kt0, bs))
                    kt0 += bs

                def emit_scores(b, hh):
                    kt0, bs = batch_of[b]
                    base = slice(hh*64, (hh+1)*64)
                    if b >= 1:
                        p.wait("tensor", "s_act", exp_done.get((b-1, hh), 0))
                    for j in range(bs):
                        kt = kt0 + j
                        bank = hh*3 + j
                        p.run("tensor",
                              lambda e, base=base, kt=kt, bank=bank, slot=slot, qsl=qsl: e.matmul(
                                  ps[:, bank*512:(bank+1)*512],
                                  kbuf[base, slot*S + kt*P: slot*S + (kt+1)*P],
                                  qbuf[base, qsl],
                                  start=True, stop=True),
                              inc="s_pe" if j == bs-1 else None)
                    sc_done[(b, hh)] = p.cnt("s_pe")

                def emit_exp(b, hh):
                    kt0, bs = batch_of[b]
                    par = b % 2
                    po = par*2*3*512 + hh*3*512
                    p.wait("scalar", "s_pe", sc_done[(b, hh)])
                    if b >= 2:
                        p.wait("scalar", "s_pe", av_done.get((b-2, hh), 0))
                    p.run("scalar", lambda e, po=po, hh=hh, bs=bs: e.activation(
                        out=ptb[:, po:po + bs*512],
                        in_=ps[:, hh*3*512: hh*3*512 + bs*512],
                        func=AF.Exp, scale=SC0), inc="s_act")
                    exp_done[(b, hh)] = p.cnt("s_act")

                def emit_av(b, hh):
                    kt0, bs = batch_of[b]
                    par = b % 2
                    fbank = 6 + hh
                    p.wait("tensor", "s_act", exp_done[(b, hh)])
                    for j in range(bs):
                        kt = kt0 + j
                        po = par*2*3*512 + hh*3*512 + j*512
                        p.run("tensor",
                              lambda e, kt=kt, po=po, slot=slot, fbank=fbank, hh=hh: e.matmul(
                                  ps[0:65, fbank*512:(fbank+1)*512],
                                  vbuf[:, slot*64*130 + kt*130 + hh*65: slot*64*130 + kt*130 + hh*65 + 65],
                                  ptb[:, po:po+512],
                                  start=(kt == 0), stop=(kt == 63)),
                              inc="s_pe" if j == bs-1 else None)
                    av_done[(b, hh)] = p.cnt("s_pe")

                sc_done = {}
                nb = len(NBATCH)
                for b in range(nb):
                    emit_scores(b, 0)
                    emit_exp(b, 0)
                    emit_scores(b, 1)
                    emit_exp(b, 1)
                    if b == 0:
                        p.wait("tensor", "s_dve", norm_done_dve)
                    if b >= 1:
                        emit_av(b-1, 0)
                        emit_av(b-1, 1)
                emit_av(nb-1, 0)
                emit_av(nb-1, 1)
                kv_used_pe[hp] = p.cnt("s_pe")
                p.wait("vector", "s_pe", av_done[(len(NBATCH)-1, 1)])
                p.selfsync("vector")
                p.run("vector", lambda e: e.reciprocal(
                    rec[64:65, 0:512], ps[64:65, 6*512:6*512+512]), inc="s_dve")
                p.run("vector", lambda e: e.reciprocal(
                    rec[64:65, 512:1024], ps[64:65, 7*512:7*512+512]), inc="s_dve")
                p.run("vector", lambda e: e.tensor_copy(
                    out=tmp[0:64, 0:512], in_=ps[0:64, 6*512:6*512+512]),
                    inc="s_dve")
                p.run("vector", lambda e: e.tensor_copy(
                    out=tmp[0:64, 512:1024], in_=ps[0:64, 7*512:7*512+512]),
                    inc="s_dve")
                rec_done = p.cnt("s_dve")
                p.wait("tensor", "s_dve", rec_done)
                p.run("tensor", lambda e: e.matmul(
                    ps[0:64, 6*512:6*512+512], ones_blk[64:65, 0:64],
                    rec[64:65, 0:512], start=True, stop=True), inc="s_pe")
                p.run("tensor", lambda e: e.matmul(
                    ps[0:64, 7*512:7*512+512], ones_blk[64:65, 0:64],
                    rec[64:65, 512:1024], start=True, stop=True), inc="s_pe")
                bc_done = p.cnt("s_pe")
                p.wait("vector", "s_pe", bc_done)
                p.run("vector", lambda e, asl=asl: e.tensor_tensor(
                    out=attnT[0:64, asl], in0=tmp[0:64, 0:512],
                    in1=ps[0:64, 6*512:6*512+512], op=ALU.mult), inc="s_dve")
                p.run("vector", lambda e: e.tensor_tensor(
                    out=mvin[0:64, 0:512], in0=tmp[0:64, 512:1024],
                    in1=ps[0:64, 7*512:7*512+512], op=ALU.mult), inc="s_dve")
                mv_done = p.cnt("s_dve")
                p.wait("tensor", "s_dve", mv_done)
                p.run("tensor", lambda e: e.matmul(
                    ps[64:128, 7*512:7*512+512], id64[0:64, 0:64],
                    mvin[0:64, 0:512],
                    start=True, stop=True, tile_position=(0, 64)), inc="s_pe")
                mvmm = p.cnt("s_pe")
                p.wait("vector", "s_pe", mvmm)
                p.run("vector", lambda e, asl=asl: e.tensor_copy(
                    out=attnT[64:128, asl], in_=ps[64:128, 7*512:7*512+512]),
                    inc="s_dve")
                norm_done_dve = p.cnt("s_dve")
        att_done_dve = norm_done_dve
        att_done_pe = p.cnt("s_pe")
        if upto <= 1:
            if dump == "attnT":
                dump_tiles([attnT[:, t*T:(t+1)*T] for t in range(DT)])
            finish()
            return nc

        r1_tiles = [r1[:, t*T:(t+1)*T] for t in range(DT)]
        p.wait("sync", "s_pe", att_done_pe)
        p.wait("gpsimd", "s_dve", att_done_dve)
        for E_ in range(DT):
            p.run("sync", lambda e, E_=E_: e.dma_start(
                out=r1_tiles[E_], in_=xT[E_*P:(E_+1)*P, :]), inc="s_ld", dma=True)
        xt_ld = p.cnt("s_ld")
        p.wait("tensor", "s_ld", xt_ld)
        p.wait("vector", "s_ld", xt_ld)
        evac_dve = {}
        cb_done = {}
        for E_ in range(DT):
            slot = E_ % 2
            wsem = "s_w%d" % slot
            if E_ >= 2:
                p.wait("gpsimd", "s_dve", evac_dve[E_ - 2])
            p.run("gpsimd", lambda e, E_=E_, slot=slot: e.dma_start(
                out=wstrip[:, slot*DT*P:(slot+1)*DT*P], in_=Wo_s[E_]),
                inc=wsem, dma=True)
            p.wait("tensor", wsem, p.cnt(wsem))
            if E_ == 0:
                p.wait("tensor", "s_dve", att_done_dve)
            if E_ >= 1:
                p.wait("tensor", "s_dve", cb_done[E_ - 1])
            for t in range(DT):
                p.run("tensor", lambda e, slot=slot, t=t: e.matmul(
                    ps[:, 0:1],
                    wstrip[:, slot*DT*P + t*P: slot*DT*P + (t+1)*P],
                    bvb[:, t:t+1], start=(t == 0), stop=(t == DT-1)),
                    inc="s_pe" if t == DT-1 else None)
            p.wait("vector", "s_pe")
            p.run("vector", lambda e, E_=E_: e.tensor_scalar(
                out=cb[:, E_:E_+1], in0=ps[:, 0:1],
                scalar1=bo[:, E_:E_+1], scalar2=None, op0=ALU.add), inc="s_dve")
            cb_done[E_] = p.cnt("s_dve")
            if E_ >= 2:
                p.wait("tensor", "s_dve", evac_dve[E_ - 2])
            for ch in range(2):
                bank = 4 + (E_ % 2)*2 + ch
                for t in range(DT):
                    p.run("tensor", lambda e, slot=slot, t=t, ch=ch, bank=bank: e.matmul(
                        ps[:, bank*512:(bank+1)*512],
                        wstrip[:, slot*DT*P + t*P: slot*DT*P + (t+1)*P],
                        attnT[:, t*T + ch*512: t*T + (ch+1)*512],
                        start=(t == 0), stop=(t == DT-1)),
                        inc="s_pe" if t == DT-1 else None)
            p.wait("vector", "s_pe")
            p.wait("vector", "s_dve", cb_done[E_])
            for ch in range(2):
                bank = 4 + (E_ % 2)*2 + ch
                sl = slice(E_*T + ch*512, E_*T + (ch+1)*512)
                p.run("vector", lambda e, E_=E_, sl=sl, bank=bank: e.scalar_tensor_tensor(
                    out=r1[:, sl], in0=ps[:, bank*512:(bank+1)*512],
                    scalar=cb[:, E_:E_+1], in1=r1[:, sl],
                    op0=ALU.add, op1=ALU.add), inc="s_dve")
            evac_dve[E_] = p.cnt("s_dve")
            p.wait("sync", "s_dve", evac_dve[E_])
            p.run("sync", lambda e, E_=E_: e.dma_start(
                out=r1_d[E_*P:(E_+1)*P, :], in_=r1[:, E_*T:(E_+1)*T]),
                inc="s_st0", dma=True)
        wo_done_dve = p.cnt("s_dve")
        r1_stored = p.cnt("s_st0")
        if upto <= 2:
            if dump == "r1":
                dump_tiles(r1_tiles)
            finish()
            return nc

        p.wait("vector", "s_pe", p.cnt("s_pe"))
        p.wait("tensor", "s_dve", wo_done_dve)
        emit_ln(p, r1_tiles, None, xbf1, ps, rows, ones_col, ones_row,
                su_ptr=suug, dst32=xn1f)
        xn1_done = p.cnt("s_dve")
        if upto <= 3:
            if dump == "xn1f":
                dump_tiles([xn1f[:, t*T:(t+1)*T] for t in range(DT)])
            finish()
            return nc

        slots = {t: xn1f[:, t*T:(t+1)*T] for t in range(DT)}
        free_slot = fscr[:, 0:T]
        p.wait("vector", "s_dve", xn1_done)
        for sbit in (1, 2, 4, 8):
            last = (sbit == 8)
            for a in range(DT):
                if a & sbit:
                    continue
                b_ = a | sbit
                Aap, Bap = slots[a], slots[b_]
                p.selfsync("vector")
                if not last:
                    newA = free_slot
                    p.run("vector", lambda e, newA=newA, Aap=Aap, Bap=Bap: e.tensor_tensor(
                        out=newA, in0=Aap, in1=Bap, op=ALU.add), inc="s_dve")
                    p.run("vector", lambda e, Aap=Aap, Bap=Bap: e.tensor_tensor(
                        out=Bap, in0=Aap, in1=Bap, op=ALU.subtract), inc="s_dve")
                    free_slot = Aap
                    slots[a] = newA
                else:
                    p.run("vector", lambda e, a=a, Aap=Aap, Bap=Bap: e.tensor_tensor(
                        out=x1[:, a*T:(a+1)*T], in0=Aap, in1=Bap, op=ALU.add),
                        inc="s_dve")
                    p.run("vector", lambda e, b_=b_, Aap=Aap, Bap=Bap: e.tensor_tensor(
                        out=x1[:, b_*T:(b_+1)*T], in0=Aap, in1=Bap, op=ALU.subtract),
                        inc="s_dve")
        h16_done = p.cnt("s_dve")
        p.wait("tensor", "s_dve", h16_done)
        h128_act = {}
        for t in range(DT):
            for ch in range(2):
                g = t*2 + ch
                bank = g % 2
                if g >= 2:
                    p.wait("tensor", "s_act", h128_act[g-2])
                p.run("tensor", lambda e, t=t, ch=ch, bank=bank: e.matmul(
                    ps[:, bank*512:(bank+1)*512], H128[:, :],
                    x1[:, t*T + ch*512: t*T + (ch+1)*512],
                    start=True, stop=True), inc="s_pe")
                p.wait("scalar", "s_pe", p.cnt("s_pe"))
                p.run("scalar", lambda e, t=t, ch=ch, bank=bank: e.activation(
                    out=x1[:, t*T + ch*512: t*T + (ch+1)*512],
                    in_=ps[:, bank*512:(bank+1)*512], func=AF.Identity),
                    inc="s_act")
                h128_act[g] = p.cnt("s_act")
        x1_done_act = p.cnt("s_act")
        if upto <= 4:
            if dump == "x1":
                dump_tiles([x1[:, t*T:(t+1)*T] for t in range(DT)])
            finish()
            return nc

        p.wait("tensor", "s_act", x1_done_act)
        p.wait("vector", "s_st0", r1_stored)
        p.wait("gpsimd", "s_dve", wo_done_dve)
        ug_stt = {}
        ug_silu = {}
        wug_used = {}
        SBITS = (1, 2, 4, 8, 16, 32)
        level = {}
        done_pairs = set()

        def try_butterflies():
            progressed = True
            while progressed:
                progressed = False
                for s_idx, sbit in enumerate(SBITS):
                    for a in range(64):
                        if (a & sbit) or ((s_idx, a) in done_pairs):
                            continue
                        b_ = a | sbit
                        if level.get(a, -1) == s_idx and level.get(b_, -1) == s_idx:
                            Aap = hid[:, a*T:(a+1)*T]
                            Bap = hid[:, b_*T:(b_+1)*T]
                            p.selfsync("vector")
                            p.run("vector", lambda e, Aap=Aap, Bap=Bap: e.tensor_tensor(
                                out=Aap, in0=Aap, in1=Bap, op=ALU.add), inc="s_dve")
                            p.selfsync("vector")
                            p.run("vector", lambda e, Aap=Aap, Bap=Bap: e.scalar_tensor_tensor(
                                out=Bap, in0=Bap, scalar=-2.0, in1=Aap,
                                op0=ALU.mult, op1=ALU.add), inc="s_dve")
                            done_pairs.add((s_idx, a))
                            level[a] = level[b_] = s_idx + 1
                            progressed = True

        for E_ in range(p5_tiles):
            for kind in range(2):
                si = E_*2 + kind
                slot = si % 4
                wsem = "s_w%d" % slot
                if si >= 4:
                    p.wait("gpsimd", "s_pe", wug_used[si - 4])
                src = WupT_s if kind == 0 else WgT_s
                p.run("gpsimd", lambda e, src=src, E_=E_, slot=slot: e.dma_start(
                    out=wug[:, slot*DT*P:(slot+1)*DT*P], in_=src[E_]),
                    inc=wsem, dma=True)
                p.wait("tensor", wsem, p.cnt(wsem))
            for ch in range(2):
                gg = E_*2 + ch
                g = gg % 4
                if gg >= 4:
                    p.wait("tensor", "s_dve", ug_stt[gg - 4])
                    p.wait("tensor", "s_act", ug_silu[gg - 4])
                for kind in range(2):
                    slot = (E_*2 + kind) % 4
                    bank = g*2 + kind
                    for t in range(DT):
                        p.run("tensor", lambda e, slot=slot, t=t, ch=ch, bank=bank: e.matmul(
                            ps[:, bank*512:(bank+1)*512],
                            wug[:, slot*DT*P + t*P: slot*DT*P + (t+1)*P],
                            x1[:, t*T + ch*512: t*T + (ch+1)*512],
                            start=(t == 0), stop=(t == DT-1)),
                            inc="s_pe" if t == DT-1 else None)
                wug_used[E_*2 + 1] = wug_used[E_*2] = p.cnt("s_pe")
                p.wait("scalar", "s_pe", p.cnt("s_pe"))
                sslot = gg % 4
                p.run("scalar", lambda e, g=g, E_=E_, sslot=sslot: e.activation(
                    out=silu[:, sslot*512:(sslot+1)*512],
                    in_=ps[:, (g*2+1)*512:(g*2+2)*512],
                    func=AF.Silu, scale=gsc[:, E_:E_+1]), inc="s_act")
                ug_silu[gg] = p.cnt("s_act")
                p.wait("vector", "s_act", ug_silu[gg])
                hsl = slice(E_*T + ch*512, E_*T + (ch+1)*512)
                p.run("vector", lambda e, g=g, E_=E_, hsl=hsl, sslot=sslot: e.scalar_tensor_tensor(
                    out=hid[:, hsl], in0=ps[:, (g*2)*512:(g*2+1)*512],
                    scalar=usc[:, E_:E_+1], in1=silu[:, sslot*512:(sslot+1)*512],
                    op0=ALU.mult, op1=ALU.mult), inc="s_dve")
                ug_stt[gg] = p.cnt("s_dve")
            level[E_] = 0
            try_butterflies()
        ug_done = p.cnt("s_dve")
        ug_pe_done = p.cnt("s_pe")
        if upto <= 5:
            finish()
            return nc

        hslots = {t: hid[:, t*T:(t+1)*T] for t in range(64)}
        assert all(level[t] == len(SBITS) for t in range(64)), "wave incomplete"
        h64_done = p.cnt("s_dve")
        p.wait("tensor", "s_dve", h64_done)
        h128b_act = {}
        for t in range(64):
            for ch in range(2):
                g = t*2 + ch
                bank = g % 2
                if g >= 2:
                    p.wait("tensor", "s_act", h128b_act[g-2])
                p.run("tensor", lambda e, t=t, ch=ch, bank=bank: e.matmul(
                    ps[:, bank*512:(bank+1)*512], H128[:, :],
                    hid[:, t*T + ch*512: t*T + (ch+1)*512],
                    start=True, stop=True), inc="s_pe")
                p.wait("scalar", "s_pe", p.cnt("s_pe"))
                p.run("scalar", lambda e, t=t, ch=ch, bank=bank: e.activation(
                    out=hid[:, t*T + ch*512: t*T + (ch+1)*512],
                    in_=ps[:, bank*512:(bank+1)*512], func=AF.Identity),
                    inc="s_act")
                h128b_act[g] = p.cnt("s_act")
        x2_done_act = p.cnt("s_act")
        if upto <= 6:
            if dump == "x2":
                dump_tiles([hslots[t] for t in range(64)])
            finish()
            return nc

        p.wait("gpsimd", "s_pe", ug_pe_done)
        p.wait("sync", "s_st0", r1_stored)
        p.wait("sync", "s_dve", ug_done)
        p.wait("tensor", "s_act", x2_done_act)
        dn_stt = {}
        dbuf_used = {}
        for E_ in range(DT):
            slot = E_ % 2
            wsem = "s_w%d" % slot
            if E_ >= 2:
                p.wait("gpsimd", "s_pe", dbuf_used[E_ - 2])
            p.run("gpsimd", lambda e, E_=E_, slot=slot: e.dma_start(
                out=dbuf[:, slot*64*P:(slot+1)*64*P], in_=WdT_s[E_]),
                inc=wsem, dma=True)
            p.wait("tensor", wsem, p.cnt(wsem))
            if E_ >= 2:
                p.wait("tensor", "s_dve", dn_stt[E_ - 2])
            for ch in range(2):
                bank = 4 + (E_ % 2)*2 + (ch % 2)
                rdsem = "s_rd%d" % (ch % 2)
                p.wait("sync", "s_dve", dn_stt.get(E_ - 1, 0))
                p.run("sync", lambda e, E_=E_, ch=ch: e.dma_start(
                    out=rch[:, (ch % 2)*512:(ch % 2 + 1)*512],
                    in_=r1_d[E_*P:(E_+1)*P, ch*512:(ch+1)*512]),
                    inc=rdsem, dma=True)
                for t in range(64):
                    p.run("tensor", lambda e, slot=slot, t=t, ch=ch, bank=bank: e.matmul(
                        ps[:, bank*512:(bank+1)*512],
                        dbuf[:, slot*64*P + t*P: slot*64*P + (t+1)*P],
                        hslots[t][:, ch*512:(ch+1)*512],
                        start=(t == 0), stop=(t == 63)),
                        inc="s_pe" if t == 63 else None)
                dbuf_used[E_] = p.cnt("s_pe")
                p.wait("vector", "s_pe")
                p.wait("vector", rdsem, p.cnt(rdsem))
                stsem = "s_st%d" % (ch % 2)
                p.wait("vector", stsem, p.cnt(stsem))
                p.run("vector", lambda e, E_=E_, ch=ch, bank=bank: e.scalar_tensor_tensor(
                    out=stg[:, (ch % 2)*512:(ch % 2+1)*512],
                    in0=ps[:, bank*512:(bank+1)*512],
                    scalar=dsc[:, E_:E_+1], in1=rch[:, (ch % 2)*512:(ch % 2+1)*512],
                    op0=ALU.mult, op1=ALU.add), inc="s_dve")
                dn_stt[E_] = p.cnt("s_dve")
                p.wait("sync", "s_dve", dn_stt[E_])
                p.run("sync", lambda e, E_=E_, ch=ch: e.dma_start(
                    out=r2_o[E_*P:(E_+1)*P, ch*512:(ch+1)*512],
                    in_=stg[:, (ch % 2)*512:(ch % 2+1)*512]),
                    inc=stsem, dma=True)
        dn_done = p.cnt("s_dve")
        r2_st0, r2_st1 = p.cnt("s_st0"), p.cnt("s_st1")
        if upto <= 7:
            finish()
            return nc

        p.wait("sync", "s_pe", p.cnt("s_pe"))
        p.wait("sync", "s_st0", r2_st0)
        p.wait("sync", "s_st1", r2_st1)
        r2_tiles = [r2[:, t*T:(t+1)*T] for t in range(DT)]
        for t in range(DT):
            ld(r2_tiles[t], r2_o[t*P:(t+1)*P, :])
        p.wait("vector", "s_ld", p.cnt("s_ld"))
        p.wait("vector", "s_pe", p.cnt("s_pe"))
        p.wait("tensor", "s_dve", dn_done)
        emit_ln(p, r2_tiles, xn2, xbf2, ps, rows, ones_col, ones_row)
        xn2_done = p.cnt("s_dve")
        p.wait("tensor", "s_dve", xn2_done)
        if upto <= 8:
            finish()
            return nc

        strip_plan = [("q", e_) for e_ in range(DT)] + [("k", e_) for e_ in range(DT)]
        strip_done_dve = {}
        stage_out_cnt = {}
        group_evac_dve = {}
        for si, (kind, E_) in enumerate(strip_plan):
            slot = si % 2
            wsem = "s_w%d" % slot
            src = wq1_s if kind == "q" else wk1_s
            dst = q1T_o if kind == "q" else k1T_o
            if si >= 2:
                p.wait("gpsimd", "s_dve", strip_done_dve[si - 2])
            else:
                p.wait("gpsimd", "s_dve", xn2_done)
            p.run("gpsimd", lambda e, src=src, E_=E_, slot=slot: e.dma_start(
                out=wqk[:, slot*DT*P:(slot+1)*DT*P], in_=src[E_]),
                inc=wsem, dma=True)
            p.wait("tensor", wsem, p.cnt(wsem))
            for ch in range(2):
                g = si * 2 + ch
                bank = 4 + (g % 2)
                poff = bank * 512
                if g >= 2:
                    p.wait("tensor", "s_dve", group_evac_dve[g - 2])
                for t in range(DT):
                    p.run("tensor", lambda e, slot=slot, t=t, ch=ch, poff=poff: e.matmul(
                        ps[:, poff:poff+512],
                        wqk[:, slot*DT*P + t*P: slot*DT*P + (t+1)*P],
                        xn2[:, t*T + ch*512: t*T + (ch+1)*512],
                        start=(t == 0), stop=(t == DT-1)),
                        inc="s_pe" if t == DT-1 else None)
                p.wait("vector", "s_pe")
                sslot = g % 2
                osem = "s_st%d" % sslot
                if g >= 2:
                    p.wait("vector", osem, stage_out_cnt[g - 2])
                p.run("vector", lambda e, poff=poff, sslot=sslot: e.tensor_copy(
                    out=stgb[:, sslot*512:(sslot+1)*512],
                    in_=ps[:, poff:poff+512]), inc="s_dve")
                group_evac_dve[g] = p.cnt("s_dve")
                p.wait("sync", "s_dve", group_evac_dve[g])
                p.run("sync", lambda e, dst=dst, E_=E_, ch=ch, sslot=sslot: e.dma_start(
                    out=dst[E_*P:(E_+1)*P, ch*512:(ch+1)*512],
                    in_=stgb[:, sslot*512:(sslot+1)*512]),
                    inc=osem, dma=True)
                stage_out_cnt[g] = p.cnt(osem)
            strip_done_dve[si] = p.cnt("s_dve")
        qk1_done_pe = p.cnt("s_pe")

        p.wait("gpsimd", "s_dve", xn2_done)
        p.wait("gpsimd", "s_pe", qk1_done_pe)
        for t in range(DT):
            p.run("gpsimd", lambda e, t=t: e.dma_start(
                out=wv1[:, t*D:(t+1)*D], in_=wv1_r[t]), inc="s_wv", dma=True)
        p.wait("tensor", "s_wv", p.cnt("s_wv"))
        v_evac_dve = {}
        v_out_cnt = {}
        for tt in range(8):
            vslot = tt % 2
            if tt >= 1:
                p.wait("tensor", "s_dve", v_evac_dve[tt - 1])
            for ec in range(4):
                poff = ec * 512
                for t in range(DT):
                    p.run("tensor", lambda e, t=t, tt=tt, ec=ec, poff=poff: e.matmul(
                        ps[:, poff:poff+512],
                        xn2[:, t*T + tt*128: t*T + tt*128 + 128],
                        wv1[:, t*D + ec*512: t*D + (ec+1)*512],
                        start=(t == 0), stop=(t == DT-1)),
                        inc="s_pe" if t == DT-1 else None)
            p.wait("vector", "s_pe")
            vsem = "s_st%d" % vslot
            if tt >= 2:
                p.wait("vector", vsem, v_out_cnt[tt - 2])
            for ec in range(4):
                p.run("vector", lambda e, vslot=vslot, ec=ec: e.tensor_copy(
                    out=vstage[:, vslot*D + ec*512: vslot*D + (ec+1)*512],
                    in_=ps[:, ec*512:(ec+1)*512]), inc="s_dve")
            v_evac_dve[tt] = p.cnt("s_dve")
            p.wait("sync", "s_dve", v_evac_dve[tt])
            p.run("sync", lambda e, tt=tt, vslot=vslot: e.dma_start(
                out=v1_o[tt*128:(tt+1)*128, :],
                in_=vstage[:, vslot*D:(vslot+1)*D]), inc=vsem, dma=True)
            v_out_cnt[tt] = p.cnt(vsem)

        for sname in ("s_st0", "s_st1"):
            p.wait("gpsimd", sname, p.cnt(sname))
        p.emit(st)
    return nc



def col_strips(w, tin, tout):
    return np.ascontiguousarray(
        w.reshape(tin, P, tout, P).transpose(2, 1, 0, 3).reshape(tout, P, tin*P))


def row_strips(w, tin):
    return np.ascontiguousarray(np.asarray(w).reshape(tin, P, -1))


def vec_tiles(v):
    return np.ascontiguousarray(np.asarray(v).reshape(-1, P).T)


def build_launch_c(upto=9, sum_pack=1):
    nc = bass.Bass("TRN2", target_bir_lowering=False)
    dram = lambda n, s, d, k="ExternalInput": nc.dram_tensor(n, s, d, kind=k).ap()
    r2T = dram("r2T", [D, T], F32)
    q1T_i = dram("q1T_i", [D, T], BF16)
    k1Tp = dram("k1Tp", [16, P, S], BF16)
    v1p = dram("v1p", [16, P, 64*128], BF16)
    tri_i = dram("tri", [P, 8*P], BF16)
    wout_s = dram("wout_s", [DT, P, DT*P], F32)
    wqg_s = dram("wqg_s", [HT1, P, DT*P], I32)
    wqu_s = dram("wqu_s", [HT1, P, DT*P], I32)
    wqd_s = dram("wqd_s", [DT, P, HT1*P], I32)
    sg_i = dram("sg", [P, HT1], F32)
    su2_i = dram("su2", [P, HT1], F32)
    sd_i = dram("sd", [P, DT], F32)
    yT_o = dram("yT_o", [D, T], F32, "ExternalOutput")
    r3_d = nc.dram_tensor("r3_d", [D, T], F32, kind="Internal").ap()

    K = 1024
    with ExitStack() as st:
        E = st.enter_context
        p = Prog(nc)
        for sname in ("s_ld", "s_kv0", "s_kv1", "s_w0", "s_w1", "s_w2", "s_w3",
                      "s_rd0", "s_rd1", "s_pe", "s_act", "s_dve", "s_cast",
                      "s_st0", "s_st1"):
            p.sem(sname, st)

        rows = E(nc.sbuf_tensor("rows", [1, 3072], F32))
        ones_col = E(nc.sbuf_tensor("ones_col", [P, 1], BF16))
        ones_row = E(nc.sbuf_tensor("ones_row", [1, P], F32))
        tri = E(nc.sbuf_tensor("tri_s", [P, 8*P], BF16))
        onesf = E(nc.sbuf_tensor("onesf_s", [P, P], F32))
        recb = E(nc.sbuf_tensor("recb_s", [P, 512], F32))
        sg = E(nc.sbuf_tensor("sg_s", [P, HT1], F32))
        su2 = E(nc.sbuf_tensor("su2_s", [P, HT1], F32))
        sd = E(nc.sbuf_tensor("sd_s", [P, DT], F32))
        ps = E(nc.psum_tensor("ps8", [P, 4096], F32))
        M = Mem(nc, st, 184)

        attnT = M.at(0, DT*T, BF16)
        kbuf = M.at(32*K, 2*S, BF16)
        vbuf = M.at(64*K, 2*64*P, BF16)
        ptb = M.at(96*K, 2*3*512, BF16)
        qbuf = M.at(104*K, 2*1024, BF16)
        tmp = M.at(108*K, 512, BF16)
        rec = M.at(110*K, 512, F32)
        r3 = M.at(32*K, DT*T, F32)
        wstrip = M.at(96*K, 2*DT*P, BF16)
        xbf3 = M.at(96*K, DT*T, BF16)
        xn3 = M.at(0, DT*T, BF16)
        gu = M.at(32*K, HT1*T, BF16)
        wug4 = M.at(128*K, 4*DT*P, BF16)
        silu = M.at(144*K, 4*512, BF16)
        dbuf = M.at(120*K, 2*HT1*P, BF16)
        rch = M.at(166*K, 2*512, F32)
        stg = M.at(170*K, 2*512, F32)

        def ld(dst, src):
            p.run("sync", lambda e, dst=dst, src=src: e.dma_start(
                out=dst, in_=src), inc="s_ld", dma=True)
        ld(tri[:], tri_i)
        ld(sg[:], sg_i)
        ld(su2[:], su2_i)
        ld(sd[:], sd_i)
        p.run("vector", lambda e: e.memset(ones_col[:], 1.0), inc="s_dve")
        p.run("vector", lambda e: e.memset(ones_row[:], 1.0), inc="s_dve")
        p.run("vector", lambda e: e.memset(onesf[:], 1.0), inc="s_dve")
        const_done = p.cnt("s_dve")
        const_ld = p.cnt("s_ld")
        p.wait("tensor", "s_ld", const_ld)
        p.wait("tensor", "s_dve", const_done)
        p.wait("vector", "s_ld", const_ld)

        kv_used_pe = {}
        norm_done_dve = 0
        for h in range(16):
            slot = h % 2
            kvsem = "s_kv%d" % slot
            if h >= 2:
                p.wait("sync", "s_pe", kv_used_pe[h - 2])
            p.run("sync", lambda e, h=h, slot=slot: e.dma_start(
                out=kbuf[:, slot*S:(slot+1)*S], in_=k1Tp[h]), inc=kvsem, dma=True)
            p.run("sync", lambda e, h=h, slot=slot: e.dma_start(
                out=vbuf[:, slot*64*P:(slot+1)*64*P], in_=v1p[h]),
                inc=kvsem, dma=True)
            p.run("sync", lambda e, h=h, slot=slot: e.dma_start(
                out=qbuf[:, slot*1024:(slot+1)*1024], in_=q1T_i[h*P:(h+1)*P, :]),
                inc=kvsem, dma=True)
            p.wait("tensor", kvsem, p.cnt(kvsem))
            for ch in range(2):
                qsl = slice(slot*1024 + ch*512, slot*1024 + (ch+1)*512)
                asl = slice(h*T + ch*512, h*T + (ch+1)*512)
                jmax = min(4*ch + 4, 8)
                kts_full = [(ck, jk) for ck in range(8) for jk in range(min(4*ch, 8))]
                kts_part = [(ck, jk) for ck in range(8) for jk in range(4*ch, jmax)]
                kts = kts_full + kts_part
                nfull = len(kts_full)
                batches = []
                i0 = 0
                while i0 < len(kts):
                    bs = min(3, len(kts) - i0)
                    batches.append(kts[i0:i0+bs])
                    i0 += bs
                exp_done = {}
                av_done = {}
                msk_done = {}
                sc_done = {}
                nkts = len(kts)
                nsub = min(sum_pack, nkts)
                sub_first = {r: min(i for i in range(nkts) if i % nsub == r)
                             for r in range(nsub)}
                sub_last = {r: max(i for i in range(nkts) if i % nsub == r)
                            for r in range(nsub)}

                def emit_scores(b):
                    bk = batches[b]
                    par = b % 2
                    if b >= 2:
                        p.wait("tensor", "s_act", exp_done.get(b-2, 0))
                    for j, (ck, jk) in enumerate(bk):
                        kt = ck*8 + jk
                        bank = par*3 + j
                        p.run("tensor", lambda e, kt=kt, bank=bank, slot=slot, qsl=qsl: e.matmul(
                            ps[:, bank*512:(bank+1)*512],
                            kbuf[:, slot*S + kt*P: slot*S + (kt+1)*P],
                            qbuf[:, qsl], start=True, stop=True),
                            inc="s_pe" if j == len(bk)-1 else None)
                    sc_done[b] = p.cnt("s_pe")

                def emit_exp_mask(b):
                    bk = batches[b]
                    par = b % 2
                    po0 = par*3*512
                    p.wait("scalar", "s_pe", sc_done[b])
                    if b >= 2:
                        p.wait("scalar", "s_pe", av_done.get(b-2, 0))
                    p.run("scalar", lambda e, po0=po0, par=par, n=len(bk): e.activation(
                        out=ptb[:, po0:po0 + n*512], in_=ps[:, par*3*512: par*3*512 + n*512],
                        func=AF.Exp, scale=SC1), inc="s_act")
                    exp_done[b] = p.cnt("s_act")
                    any_mask = False
                    for j, (ck, jk) in enumerate(bk):
                        if (ck, jk) in kts_part:
                            m = jk - 4*ch
                            if not any_mask:
                                p.wait("vector", "s_act", exp_done[b])
                                any_mask = True
                            for qq in range(m):
                                p.run("vector", lambda e, po0=po0, j=j, qq=qq: e.memset(
                                    ptb[:, po0 + j*512 + qq*128: po0 + j*512 + (qq+1)*128],
                                    0.0), inc="s_dve")
                            if m < 4:
                                p.run("vector", lambda e, po0=po0, j=j, m=m, ck=ck: e.tensor_tensor(
                                    out=ptb[:, po0 + j*512 + m*128: po0 + j*512 + (m+1)*128],
                                    in0=ptb[:, po0 + j*512 + m*128: po0 + j*512 + (m+1)*128],
                                    in1=tri[:, ck*P:(ck+1)*P], op=ALU.mult),
                                    inc="s_dve")
                    msk_done[b] = (p.cnt("s_dve"), any_mask)

                def emit_av(b):
                    bk = batches[b]
                    par = b % 2
                    po0 = par*3*512
                    cnt, any_mask = msk_done[b]
                    if any_mask:
                        p.wait("tensor", "s_dve", cnt)
                    else:
                        p.wait("tensor", "s_act", exp_done[b])
                    gi0 = sum(len(batches[x]) for x in range(b))
                    for j, (ck, jk) in enumerate(bk):
                        kt = ck*8 + jk
                        po = po0 + j*512
                        gi = gi0 + j
                        p.run("tensor", lambda e, kt=kt, po=po, slot=slot, st_=(gi == 0), last=(gi == nkts-1): e.matmul(
                            ps[0:P, 6*512:7*512],
                            vbuf[:, slot*64*P + kt*P: slot*64*P + (kt+1)*P],
                            ptb[:, po:po+512],
                            start=st_, stop=last), inc=None)
                        r_ = gi % nsub
                        cp = 32 * r_
                        tp_kw = {"tile_position": (0, cp)} if cp else {}
                        p.run("tensor", lambda e, po=po, cp=cp, st_=(gi == sub_first[r_]), last=(gi == sub_last[r_]), tp_kw=tp_kw: e.matmul(
                            ps[cp:cp+1, 7*512:7*512+512],
                            ones_col[:, 0:1], ptb[:, po:po+512],
                            start=st_, stop=last, **tp_kw),
                            inc="s_pe" if (j == len(bk)-1) else None)
                    av_done[b] = p.cnt("s_pe")

                for b in range(len(batches)):
                    emit_scores(b)
                    emit_exp_mask(b)
                    if b == 0:
                        p.wait("tensor", "s_dve", norm_done_dve)
                    if b >= 1:
                        emit_av(b-1)
                emit_av(len(batches)-1)
                kv_used_pe[h] = p.cnt("s_pe")
                p.wait("vector", "s_pe", av_done[len(batches)-1])
                p.selfsync("vector")
                p.run("vector", lambda e: e.reciprocal(
                    rec[0:1, 0:512], ps[0:1, 7*512:7*512+512]), inc="s_dve")
                p.run("vector", lambda e: e.tensor_copy(
                    out=tmp[:, 0:512], in_=ps[:, 6*512:7*512]), inc="s_dve")
                rec_done = p.cnt("s_dve")
                p.wait("tensor", "s_dve", rec_done)
                p.run("tensor", lambda e: e.matmul(
                    ps[:, 6*512:7*512], ones_row[0:1, :], rec[0:1, 0:512],
                    start=True, stop=True), inc="s_pe")
                bc_done = p.cnt("s_pe")
                p.wait("vector", "s_pe", bc_done)
                p.run("vector", lambda e, asl=asl: e.tensor_tensor(
                    out=attnT[:, asl], in0=tmp[:, 0:512],
                    in1=ps[:, 6*512:7*512], op=ALU.mult), inc="s_dve")
                norm_done_dve = p.cnt("s_dve")
        att_done_dve = norm_done_dve
        att_done_pe = p.cnt("s_pe")
        if upto <= 1:
            for sname in ("s_st0", "s_st1"):
                p.wait("gpsimd", sname, p.cnt(sname))
            p.wait("gpsimd", "s_dve", att_done_dve)
            p.emit(st)
            return nc

        r3_tiles = [r3[:, t*T:(t+1)*T] for t in range(DT)]
        p.wait("sync", "s_pe", att_done_pe)
        for E_ in range(DT):
            p.run("sync", lambda e, E_=E_: e.dma_start(
                out=r3_tiles[E_], in_=r2T[E_*P:(E_+1)*P, :]), inc="s_ld", dma=True)
        r2_ld = p.cnt("s_ld")
        p.wait("tensor", "s_ld", r2_ld)
        p.wait("vector", "s_ld", r2_ld)
        p.wait("gpsimd", "s_dve", att_done_dve)
        evac_dve = {}
        for E_ in range(DT):
            slot = E_ % 2
            wsem = "s_w%d" % slot
            if E_ >= 2:
                p.wait("gpsimd", "s_dve", evac_dve[E_ - 2])
            p.run("gpsimd", lambda e, E_=E_, slot=slot: e.dma_start(
                out=wstrip[:, slot*DT*P:(slot+1)*DT*P], in_=wout_s[E_]),
                inc=wsem, dma=True)
            p.wait("tensor", wsem, p.cnt(wsem))
            if E_ == 0:
                p.wait("tensor", "s_dve", att_done_dve)
            if E_ >= 2:
                p.wait("tensor", "s_dve", evac_dve[E_ - 2])
            for ch in range(2):
                bank = 4 + (E_ % 2)*2 + ch
                for t in range(DT):
                    p.run("tensor", lambda e, slot=slot, t=t, ch=ch, bank=bank: e.matmul(
                        ps[:, bank*512:(bank+1)*512],
                        wstrip[:, slot*DT*P + t*P: slot*DT*P + (t+1)*P],
                        attnT[:, t*T + ch*512: t*T + (ch+1)*512],
                        start=(t == 0), stop=(t == DT-1)),
                        inc="s_pe" if t == DT-1 else None)
            p.wait("vector", "s_pe")
            p.selfsync("vector")
            for ch in range(2):
                bank = 4 + (E_ % 2)*2 + ch
                sl = slice(E_*T + ch*512, E_*T + (ch+1)*512)
                p.run("vector", lambda e, sl=sl, bank=bank: e.tensor_tensor(
                    out=r3[:, sl], in0=ps[:, bank*512:(bank+1)*512],
                    in1=r3[:, sl], op=ALU.add), inc="s_dve")
            evac_dve[E_] = p.cnt("s_dve")
            p.wait("sync", "s_dve", evac_dve[E_])
            p.run("sync", lambda e, E_=E_: e.dma_start(
                out=r3_d[E_*P:(E_+1)*P, :], in_=r3[:, E_*T:(E_+1)*T]),
                inc="s_st0", dma=True)
        wo_done_dve = p.cnt("s_dve")
        r3_stored = p.cnt("s_st0")
        if upto <= 2:
            for sname in ("s_st0", "s_st1"):
                p.wait("gpsimd", sname, p.cnt(sname))
            p.emit(st)
            return nc

        p.wait("vector", "s_pe", p.cnt("s_pe"))
        p.wait("tensor", "s_dve", wo_done_dve)
        emit_ln(p, r3_tiles, xn3, xbf3, ps, rows, ones_col, ones_row)
        xn3_done = p.cnt("s_dve")
        p.wait("tensor", "s_dve", xn3_done)

        p.wait("vector", "s_st0", r3_stored)
        p.wait("gpsimd", "s_dve", xn3_done)
        ug_stt = {}
        ug_silu = {}
        wug_used = {}
        for E_ in range(HT1):
            for kind in range(2):
                si = E_*2 + kind
                slot = si % 4
                wsem = "s_w%d" % slot
                if si >= 4:
                    p.wait("gpsimd", "s_pe", wug_used[si - 4])
                src = wqg_s if kind == 0 else wqu_s
                p.run("gpsimd", lambda e, src=src, E_=E_, slot=slot: e.dma_start(
                    out=wug4[:, slot*DT*P:(slot+1)*DT*P], in_=src[E_]),
                    inc=wsem, dma=True)
                p.wait("vector", wsem, p.cnt(wsem))
                p.run("vector", lambda e, slot=slot: e.tensor_scalar(
                    out=wug4[:, slot*DT*P:(slot+1)*DT*P],
                    in0=wug4[:, slot*DT*P:(slot+1)*DT*P],
                    scalar1=-128.0, scalar2=None, op0=ALU.add), inc="s_cast")
                p.wait("tensor", "s_cast", p.cnt("s_cast"))
            for ch in range(2):
                gg = E_*2 + ch
                g = gg % 4
                if gg >= 4:
                    p.wait("tensor", "s_dve", ug_stt[gg - 4])
                    p.wait("tensor", "s_act", ug_silu[gg - 4])
                for kind in range(2):
                    slot = (E_*2 + kind) % 4
                    bank = g*2 + kind
                    for t in range(DT):
                        p.run("tensor", lambda e, slot=slot, t=t, ch=ch, bank=bank: e.matmul(
                            ps[:, bank*512:(bank+1)*512],
                            wug4[:, slot*DT*P + t*P: slot*DT*P + (t+1)*P],
                            xn3[:, t*T + ch*512: t*T + (ch+1)*512],
                            start=(t == 0), stop=(t == DT-1)),
                            inc="s_pe" if t == DT-1 else None)
                wug_used[E_*2 + 1] = wug_used[E_*2] = p.cnt("s_pe")
                p.wait("scalar", "s_pe", p.cnt("s_pe"))
                sslot = gg % 4
                p.run("scalar", lambda e, g=g, E_=E_, sslot=sslot: e.activation(
                    out=silu[:, sslot*512:(sslot+1)*512],
                    in_=ps[:, (g*2)*512:(g*2+1)*512],
                    func=AF.Silu, scale=sg[:, E_:E_+1]), inc="s_act")
                ug_silu[gg] = p.cnt("s_act")
                p.wait("vector", "s_act", ug_silu[gg])
                hsl = slice(E_*T + ch*512, E_*T + (ch+1)*512)
                p.run("vector", lambda e, g=g, E_=E_, hsl=hsl, sslot=sslot: e.scalar_tensor_tensor(
                    out=gu[:, hsl], in0=ps[:, (g*2+1)*512:(g*2+2)*512],
                    scalar=su2[:, E_:E_+1], in1=silu[:, sslot*512:(sslot+1)*512],
                    op0=ALU.mult, op1=ALU.mult), inc="s_dve")
                ug_stt[gg] = p.cnt("s_dve")
        ug_done = p.cnt("s_dve")
        if upto <= 4:
            for sname in ("s_st0", "s_st1"):
                p.wait("gpsimd", sname, p.cnt(sname))
            p.wait("gpsimd", "s_dve", ug_done)
            p.emit(st)
            return nc

        p.wait("gpsimd", "s_dve", ug_done)
        p.wait("sync", "s_dve", ug_done)
        dn_stt = {}
        dbuf_used = {}
        for E_ in range(DT):
            slot = E_ % 2
            wsem = "s_w%d" % slot
            if E_ >= 2:
                p.wait("gpsimd", "s_pe", dbuf_used[E_ - 2])
            p.run("gpsimd", lambda e, E_=E_, slot=slot: e.dma_start(
                out=dbuf[:, slot*HT1*P:(slot+1)*HT1*P], in_=wqd_s[E_]),
                inc=wsem, dma=True)
            p.wait("vector", wsem, p.cnt(wsem))
            p.run("vector", lambda e, slot=slot: e.tensor_scalar(
                out=dbuf[:, slot*HT1*P:(slot+1)*HT1*P],
                in0=dbuf[:, slot*HT1*P:(slot+1)*HT1*P],
                scalar1=-128.0, scalar2=None, op0=ALU.add), inc="s_cast")
            p.wait("tensor", "s_cast", p.cnt("s_cast"))
            if E_ >= 2:
                p.wait("tensor", "s_dve", dn_stt[E_ - 2])
            for ch in range(2):
                bank = 4 + (E_ % 2)*2 + (ch % 2)
                rdsem = "s_rd%d" % (ch % 2)
                p.wait("sync", "s_dve", dn_stt.get(E_ - 1, 0))
                p.run("sync", lambda e, E_=E_, ch=ch: e.dma_start(
                    out=rch[:, (ch % 2)*512:(ch % 2 + 1)*512],
                    in_=r3_d[E_*P:(E_+1)*P, ch*512:(ch+1)*512]),
                    inc=rdsem, dma=True)
                for t in range(HT1):
                    p.run("tensor", lambda e, slot=slot, t=t, ch=ch, bank=bank: e.matmul(
                        ps[:, bank*512:(bank+1)*512],
                        dbuf[:, slot*HT1*P + t*P: slot*HT1*P + (t+1)*P],
                        gu[:, t*T + ch*512: t*T + (ch+1)*512],
                        start=(t == 0), stop=(t == HT1-1)),
                        inc="s_pe" if t == HT1-1 else None)
                dbuf_used[E_] = p.cnt("s_pe")
                p.wait("vector", "s_pe")
                p.wait("vector", rdsem, p.cnt(rdsem))
                stsem = "s_st%d" % (ch % 2)
                p.wait("vector", stsem, p.cnt(stsem))
                p.run("vector", lambda e, E_=E_, ch=ch, bank=bank: e.scalar_tensor_tensor(
                    out=stg[:, (ch % 2)*512:(ch % 2+1)*512],
                    in0=ps[:, bank*512:(bank+1)*512],
                    scalar=sd[:, E_:E_+1], in1=rch[:, (ch % 2)*512:(ch % 2+1)*512],
                    op0=ALU.mult, op1=ALU.add), inc="s_dve")
                dn_stt[E_] = p.cnt("s_dve")
                p.wait("sync", "s_dve", dn_stt[E_])
                p.run("sync", lambda e, E_=E_, ch=ch: e.dma_start(
                    out=yT_o[E_*P:(E_+1)*P, ch*512:(ch+1)*512],
                    in_=stg[:, (ch % 2)*512:(ch % 2+1)*512]),
                    inc=stsem, dma=True)
        for sname in ("s_st0", "s_st1"):
            p.wait("gpsimd", sname, p.cnt(sname))
        p.emit(st)
    return nc


def prep_c_inputs(inputs, b_res):
    import ml_dtypes
    k1_all = np.concatenate([np.asarray(b_res[c]["k1T_o"]) for c in range(NC)],
                            axis=1)
    v1_all = np.concatenate([np.asarray(b_res[c]["v1_o"]) for c in range(NC)],
                            axis=0)
    k1Tp = np.ascontiguousarray(k1_all.reshape(DT, P, S))
    v1p = np.ascontiguousarray(
        v1_all.reshape(64, P, 16, 128).transpose(2, 1, 0, 3).reshape(16, P, 64*128))
    kl = np.arange(P)[:, None]
    ql = np.arange(P)[None, :]
    tri_incl = (kl <= ql).astype(ml_dtypes.bfloat16)
    tri_strict = (kl < ql).astype(ml_dtypes.bfloat16)
    wqg = np.asarray(inputs["L1_ffn_wq_gate"])
    wqu = np.asarray(inputs["L1_ffn_wq_up"])
    wqd = np.asarray(inputs["L1_ffn_wq_down"])
    shared = {
        "k1Tp": k1Tp,
        "v1p": v1p,
        "wout_s": col_strips(np.asarray(inputs["L1_attn_w_out"]), DT, DT),
        "wqg_s": col_strips(np.ascontiguousarray(wqg.T), DT, HT1),
        "wqu_s": col_strips(np.ascontiguousarray(wqu.T), DT, HT1),
        "wqd_s": col_strips(np.ascontiguousarray(wqd.T), HT1, DT),
        "sg": vec_tiles(inputs["L1_ffn_s_gate"]),
        "su2": vec_tiles(inputs["L1_ffn_s_up"]),
        "sd": vec_tiles(inputs["L1_ffn_s_down"]),
    }
    maps = []
    for c in range(NC):
        m = dict(shared)
        m["r2T"] = np.asarray(b_res[c]["r2_o"])
        m["q1T_i"] = np.asarray(b_res[c]["q1T_o"])
        tri = np.empty((8, P, P), ml_dtypes.bfloat16)
        for ck in range(NC):
            tri[ck] = tri_incl if ck <= c else tri_strict
        m["tri"] = np.ascontiguousarray(tri.transpose(1, 0, 2).reshape(P, 8*P))
        maps.append(m)
    return maps


def hadamard128():
    i = np.arange(P, dtype=np.uint32)
    pc = np.bitwise_count(i[:, None] & i[None, :])
    return (1.0 - 2.0 * (pc % 2)).astype(np.float32)


def prep_b_inputs(inputs, a_res):
    import ml_dtypes
    x = np.asarray(inputs["L0_attn_x"])[0].astype(np.float32, copy=False)
    kT_all = np.concatenate([np.asarray(a_res[c]["kT_o"]) for c in range(NC)],
                            axis=1)
    v_all = np.concatenate([np.asarray(a_res[c]["v_o"]) for c in range(NC)],
                           axis=0)
    kTp = np.ascontiguousarray(kT_all.reshape(DT, P, S))
    v2 = v_all.reshape(64, P, 16, 128).transpose(2, 1, 0, 3)
    vaug = np.ones((16, P, 64, 130), dtype=ml_dtypes.bfloat16)
    vaug[:, :, :, 0:64] = v2[:, :, :, 0:64]
    vaug[:, :, :, 65:129] = v2[:, :, :, 64:128]
    vaug = vaug.reshape(16, P, 64*130)
    wqkv = np.asarray(inputs["L1_attn_w_qkv"])
    shared = {
        "kTp": kTp,
        "vaug": vaug,
        "Wo_s": col_strips(np.asarray(inputs["L0_attn_Wo"]), DT, DT),
        "bo": vec_tiles(inputs["L0_attn_bo"]),
        "bv": vec_tiles(inputs["L0_attn_bv"]),
        "suug": vec_tiles(inputs["L0_ffn_SU_ug"]),
        "sudp": vec_tiles(inputs["L0_ffn_SU_dp"]),
        "wsu": vec_tiles(np.asarray(inputs["L0_ffn_Wscale_ug"])[:8192]),
        "wsg": vec_tiles(np.asarray(inputs["L0_ffn_Wscale_ug"])[8192:]),
        "wsd": vec_tiles(inputs["L0_ffn_Wscale_dp"]),
        "H128": hadamard128().astype(ml_dtypes.bfloat16),
        "id64": np.eye(64, dtype=ml_dtypes.bfloat16),
        "WupT_s": col_strips(np.ascontiguousarray(
            np.asarray(inputs["L0_ffn_W_up"]).T), DT, 64),
        "WgT_s": col_strips(np.ascontiguousarray(
            np.asarray(inputs["L0_ffn_W_gate"]).T), DT, 64),
        "WdT_s": col_strips(np.ascontiguousarray(
            np.asarray(inputs["L0_ffn_W_down"]).T), 64, DT),
        "wq1_s": col_strips(np.ascontiguousarray(wqkv[:, 0:2048]), DT, DT),
        "wk1_s": col_strips(np.ascontiguousarray(wqkv[:, 2048:4096]), DT, DT),
        "wv1_r": row_strips(np.ascontiguousarray(wqkv[:, 4096:6144]), DT),
    }
    maps = []
    for c in range(NC):
        m = dict(shared)
        m["xT"] = np.ascontiguousarray(x[c::NC].T)
        m["qT_i"] = np.asarray(a_res[c]["qT_o"])
        maps.append(m)
    return maps


def prep_a_inputs(inputs):
    x = np.asarray(inputs["L0_attn_x"])[0].astype(np.float32, copy=False)
    shared = {
        "Wq_s": col_strips(np.asarray(inputs["L0_attn_Wq"]), DT, DT),
        "Wk_s": col_strips(np.asarray(inputs["L0_attn_Wk"]), DT, DT),
        "Wv_r": row_strips(inputs["L0_attn_Wv"], DT),
        "bq": vec_tiles(inputs["L0_attn_bq"]),
        "bk": vec_tiles(inputs["L0_attn_bk"]),
    }
    maps = []
    for c in range(NC):
        m = dict(shared)
        m["xT"] = np.ascontiguousarray(x[c::NC].T)
        maps.append(m)
    return maps



_NC_CACHE = {}


def _get_nc(name, builder):
    if name not in _NC_CACHE:
        _NC_CACHE[name] = builder()
    return _NC_CACHE[name]


def kernel(**inputs):
    import os
    maps_a = prep_a_inputs(inputs)
    res_a = run_bass_kernel_spmd(_get_nc("A", build_launch_a), maps_a,
                                 core_ids=list(range(NC))).results
    maps_b = prep_b_inputs(inputs, res_a)
    res_b = run_bass_kernel_spmd(_get_nc("B", build_launch_b), maps_b,
                                 core_ids=list(range(NC))).results
    if os.environ.get("SKIP_C"):
        y = np.zeros((S, D), np.float32)
        return y[None]
    maps_c = prep_c_inputs(inputs, res_b)
    res_c = run_bass_kernel_spmd(_get_nc("C", build_launch_c), maps_c,
                                 core_ids=list(range(NC))).results
    y = np.empty((S, D), np.float32)
    for c in range(NC):
        y[c::NC] = np.asarray(res_c[c]["yT_o"]).T
    return y[None]


# version 13 (final)
# speedup vs baseline: 1.0346x; 1.0346x over previous
import math
import numpy as np
from contextlib import ExitStack

import concourse.bass as bass
import concourse.mybir as mybir
from concourse.bass_utils import run_bass_kernel_spmd

F32 = mybir.dt.float32
BF16 = mybir.dt.bfloat16
I32 = mybir.dt.int32
AF = mybir.ActivationFunctionType
ALU = mybir.AluOpType

NC = 8
P = 128
T = 1024
D = 2048
S = 8192
DT = D // P
EPS = 1e-5
SC0 = 1.0 / math.sqrt(32.0)
SC1 = 1.0 / math.sqrt(128.0)
C_UG = 1.0 / math.sqrt(2048.0)
C_DP = 1.0 / math.sqrt(8192.0)
HT0 = 64
HT1 = 44

ENGINES = ("sync", "tensor", "scalar", "vector", "gpsimd")


class Prog:

    def __init__(self, nc):
        self.nc = nc
        self.steps = {e: [] for e in ENGINES}
        self.sems = {}
        self.counts = {}

    def sem(self, name, stack):
        s = stack.enter_context(self.nc.semaphore(name))
        self.sems[name] = s
        self.counts[name] = 0

    def wait(self, eng, sname, target=None):
        t = self.counts[sname] if target is None else target
        if t <= 0:
            return
        s = self.sems[sname]
        self.steps[eng].append(lambda e, s=s, t=t: e.wait_ge(s, t))

    def run(self, eng, fn, inc=None, dma=False):
        amt = 16 if dma else 1
        if inc is not None:
            self.counts[inc] += amt
            s = self.sems[inc]

            def step(e, fn=fn, s=s, amt=amt):
                fn(e).then_inc(s, amt)

            self.steps[eng].append(step)
        else:
            self.steps[eng].append(lambda e, fn=fn: fn(e))

    def cnt(self, sname):
        return self.counts[sname]

    SELF_SEM = {"vector": "s_dve", "scalar": "s_act", "tensor": "s_pe"}

    def selfsync(self, eng):
        self.wait(eng, self.SELF_SEM[eng])

    def emit(self, stack):
        block = stack.enter_context(self.nc.Block())
        for eng in ENGINES:
            deco = getattr(block, eng)
            steps = self.steps[eng]
            if not steps:
                continue

            def body(e, steps=steps):
                for st_ in steps:
                    st_(e)

            deco(body)


def emit_ln(p, r_tiles, xn_buf, xbf_buf, ps, rows, ones_col, ones_row,
            su_ptr=None, dst32=None):
    p.selfsync("vector")
    for t in range(DT):
        p.run("vector", lambda e, t=t: e.tensor_copy(
            out=xbf_buf[:, t*T:(t+1)*T], in_=r_tiles[t]), inc="s_dve")
    p.wait("tensor", "s_dve")
    for half in range(2):
        for t in range(DT):
            p.run("tensor", lambda e, t=t, half=half: e.matmul(
                ps[0:1, half*512:(half+1)*512], ones_col[:, 0:1],
                xbf_buf[:, t*T + half*512: t*T + (half+1)*512],
                start=(t == 0), stop=(t == DT-1)),
                inc="s_pe" if t == DT-1 else None)
    sums_done = p.cnt("s_pe")
    p.wait("scalar", "s_pe", sums_done)
    for t in range(DT):
        p.run("scalar", lambda e, t=t: e.activation(
            out=xbf_buf[:, t*T:(t+1)*T], in_=xbf_buf[:, t*T:(t+1)*T],
            func=AF.Square), inc="s_act")
    p.wait("tensor", "s_act")
    for half in range(2):
        for t in range(DT):
            p.run("tensor", lambda e, t=t, half=half: e.matmul(
                ps[0:1, 1024 + half*512: 1024 + (half+1)*512], ones_col[:, 0:1],
                xbf_buf[:, t*T + half*512: t*T + (half+1)*512],
                start=(t == 0), stop=(t == DT-1)),
                inc="s_pe" if t == DT-1 else None)
    p.wait("vector", "s_pe")
    p.run("vector", lambda e: e.tensor_scalar(
        out=rows[0:1, 0:1024], in0=ps[0:1, 0:1024], scalar1=1.0/D,
        scalar2=None, op0=ALU.mult), inc="s_dve")
    p.run("vector", lambda e: e.tensor_scalar(
        out=rows[0:1, 1024:2048], in0=ps[0:1, 1024:2048], scalar1=1.0/D,
        scalar2=None, op0=ALU.mult), inc="s_dve")
    p.wait("scalar", "s_dve")
    p.run("scalar", lambda e: e.activation(
        out=rows[0:1, 2048:3072], in_=rows[0:1, 0:1024], func=AF.Square),
        inc="s_act")
    p.wait("vector", "s_act")
    p.selfsync("vector")
    p.run("vector", lambda e: e.scalar_tensor_tensor(
        out=rows[0:1, 1024:2048], in0=rows[0:1, 1024:2048], scalar=EPS,
        in1=rows[0:1, 2048:3072], op0=ALU.add, op1=ALU.subtract), inc="s_dve")
    p.wait("scalar", "s_dve")
    p.run("scalar", lambda e: e.activation(
        out=rows[0:1, 2048:3072], in_=rows[0:1, 1024:2048], func=AF.Ln),
        inc="s_act")
    p.selfsync("scalar")
    p.run("scalar", lambda e: e.activation(
        out=rows[0:1, 2048:3072], in_=rows[0:1, 2048:3072], func=AF.Exp,
        scale=-0.5), inc="s_act")
    p.wait("vector", "s_act")
    p.selfsync("vector")
    p.run("vector", lambda e: e.tensor_tensor(
        out=rows[0:1, 0:1024], in0=rows[0:1, 0:1024],
        in1=rows[0:1, 2048:3072], op=ALU.mult), inc="s_dve")
    p.wait("tensor", "s_dve")
    for half in range(2):
        p.run("tensor", lambda e, half=half: e.matmul(
            ps[:, half*512:(half+1)*512], ones_row[0:1, :],
            rows[0:1, 2048 + half*512: 2048 + (half+1)*512],
            start=True, stop=True), inc="s_pe")
        p.run("tensor", lambda e, half=half: e.matmul(
            ps[:, 1024 + half*512: 1024 + (half+1)*512], ones_row[0:1, :],
            rows[0:1, half*512: (half+1)*512],
            start=True, stop=True), inc="s_pe")
    p.wait("vector", "s_pe")
    for t in range(DT):
        for half in range(2):
            sl = slice(t*T + half*512, t*T + (half+1)*512)
            hs = slice(half*512, (half+1)*512)
            dst = dst32 if dst32 is not None else xn_buf
            p.run("vector", lambda e, t=t, sl=sl, hs=hs, dst=dst: e.tensor_tensor(
                out=dst[:, sl], in0=r_tiles[t][:, hs],
                in1=ps[:, hs], op=ALU.mult), inc="s_dve")
            p.selfsync("vector")
            p.run("vector", lambda e, sl=sl, hs=hs, dst=dst: e.tensor_tensor(
                out=dst[:, sl], in0=dst[:, sl],
                in1=ps[:, slice(1024 + hs.start, 1024 + hs.stop)],
                op=ALU.subtract), inc="s_dve")
    r_done = p.cnt("s_dve")
    if su_ptr is not None:
        p.selfsync("vector")
        dst = dst32 if dst32 is not None else xn_buf
        for t in range(DT):
            p.run("vector", lambda e, t=t, dst=dst: e.tensor_scalar(
                out=dst[:, t*T:(t+1)*T], in0=dst[:, t*T:(t+1)*T],
                scalar1=su_ptr[:, t:t+1], scalar2=None, op0=ALU.mult),
                inc="s_dve")
    return r_done


def build_launch_a():
    nc = bass.Bass("TRN2", target_bir_lowering=False)
    xT = nc.dram_tensor("xT", [D, T], F32, kind="ExternalInput").ap()
    Wq_s = nc.dram_tensor("Wq_s", [DT, P, DT*P], F32, kind="ExternalInput").ap()
    Wk_s = nc.dram_tensor("Wk_s", [DT, P, DT*P], F32, kind="ExternalInput").ap()
    Wv_r = nc.dram_tensor("Wv_r", [DT, P, D], F32, kind="ExternalInput").ap()
    bq_i = nc.dram_tensor("bq", [P, DT], F32, kind="ExternalInput").ap()
    bk_i = nc.dram_tensor("bk", [P, DT], F32, kind="ExternalInput").ap()
    qT_o = nc.dram_tensor("qT_o", [D, T], BF16, kind="ExternalOutput").ap()
    kT_o = nc.dram_tensor("kT_o", [D, T], BF16, kind="ExternalOutput").ap()
    v_o = nc.dram_tensor("v_o", [T, D], BF16, kind="ExternalOutput").ap()

    with ExitStack() as st:
        E = st.enter_context
        p = Prog(nc)
        for sname in ("s_x", "s_w0", "s_w1", "s_wv", "s_pe", "s_act", "s_dve",
                      "s_out", "s_so0", "s_so1", "s_vo0", "s_vo1"):
            p.sem(sname, st)

        xn = E(nc.sbuf_tensor("xn", [P, DT*T], BF16))
        xbf = E(nc.sbuf_tensor("xbf", [P, DT*T], BF16))
        rows = E(nc.sbuf_tensor("rows", [1, 3072], F32))
        ones_col = E(nc.sbuf_tensor("ones_col", [P, 1], BF16))
        ones_row = E(nc.sbuf_tensor("ones_row", [1, P], F32))
        bq = E(nc.sbuf_tensor("bq_s", [P, DT], F32))
        bk = E(nc.sbuf_tensor("bk_s", [P, DT], F32))
        wbuf = E(nc.sbuf_tensor("wbuf", [P, 2*DT*P], BF16))
        stage = E(nc.sbuf_tensor("stage", [P, 2*512], BF16))
        vstage = E(nc.sbuf_tensor("vstage", [P, 2*D], BF16))
        ps = E(nc.psum_tensor("ps8", [P, 4096], F32))

        r_ctx = nc.sbuf_tensor("r", [P, DT*T], F32)
        r = r_ctx.__enter__()
        r_tiles = [r[:, t*T:(t+1)*T] for t in range(DT)]

        p.run("vector", lambda e: e.memset(ones_col[:], 1.0), inc="s_dve")
        p.run("vector", lambda e: e.memset(ones_row[:], 1.0), inc="s_dve")
        for t in range(DT):
            p.run("sync", lambda e, t=t: e.dma_start(
                out=r_tiles[t], in_=xT[t*P:(t+1)*P, :]), inc="s_x", dma=True)
        p.run("gpsimd", lambda e: e.dma_start(out=bq[:], in_=bq_i[:]),
              inc="s_x", dma=True)
        p.run("gpsimd", lambda e: e.dma_start(out=bk[:], in_=bk_i[:]),
              inc="s_x", dma=True)
        for eng in ("vector", "scalar", "tensor"):
            p.wait(eng, "s_x")

        r_done = emit_ln(p, r_tiles, xn, xbf, ps, rows, ones_col, ones_row)
        xn_ready = p.cnt("s_dve")
        p.wait("tensor", "s_dve", xn_ready)

        strip_plan = [("q", e_) for e_ in range(DT)] + [("k", e_) for e_ in range(DT)]
        strip_done_dve = {}
        stage_out_cnt = {}
        group_evac_dve = {}
        for si, (kind, E_) in enumerate(strip_plan):
            slot = si % 2
            wsem = "s_w%d" % slot
            src = Wq_s if kind == "q" else Wk_s
            bias = bq if kind == "q" else bk
            dst = qT_o if kind == "q" else kT_o
            if si >= 2:
                p.wait("gpsimd", "s_dve", strip_done_dve[si - 2])
            p.run("gpsimd", lambda e, src=src, E_=E_, slot=slot: e.dma_start(
                out=wbuf[:, slot*DT*P:(slot+1)*DT*P], in_=src[E_]),
                inc=wsem, dma=True)
            p.wait("tensor", wsem, p.cnt(wsem))
            for ch in range(2):
                g = si * 2 + ch
                pband = 4 + (g % 2)
                poff = pband * 512
                if g >= 2:
                    p.wait("tensor", "s_dve", group_evac_dve[g - 2])
                for t in range(DT):
                    p.run("tensor", lambda e, slot=slot, t=t, ch=ch, poff=poff: e.matmul(
                        ps[:, poff:poff+512],
                        wbuf[:, slot*DT*P + t*P: slot*DT*P + (t+1)*P],
                        xn[:, t*T + ch*512: t*T + (ch+1)*512],
                        start=(t == 0), stop=(t == DT-1)),
                        inc="s_pe" if t == DT-1 else None)
                p.wait("vector", "s_pe")
                sslot = g % 2
                osem = "s_so%d" % sslot
                if g >= 2:
                    p.wait("vector", osem, stage_out_cnt[g - 2])
                p.run("vector", lambda e, poff=poff, sslot=sslot, bias=bias, E_=E_: e.tensor_scalar(
                    out=stage[:, sslot*512:(sslot+1)*512],
                    in0=ps[:, poff:poff+512],
                    scalar1=bias[:, E_:E_+1], scalar2=None, op0=ALU.add),
                    inc="s_dve")
                group_evac_dve[g] = p.cnt("s_dve")
                p.wait("sync", "s_dve", group_evac_dve[g])
                p.run("sync", lambda e, dst=dst, E_=E_, ch=ch, sslot=sslot: e.dma_start(
                    out=dst[E_*P:(E_+1)*P, ch*512:(ch+1)*512],
                    in_=stage[:, sslot*512:(sslot+1)*512]),
                    inc=osem, dma=True)
                stage_out_cnt[g] = p.cnt(osem)
            strip_done_dve[si] = p.cnt("s_dve")

        r_ctx.__exit__(None, None, None)
        wv_ctx = nc.sbuf_tensor("wv", [P, DT*D], BF16)
        wv = wv_ctx.__enter__()

        p.wait("gpsimd", "s_dve", r_done)
        for t in range(DT):
            p.run("gpsimd", lambda e, t=t: e.dma_start(
                out=wv[:, t*D:(t+1)*D], in_=Wv_r[t]), inc="s_wv", dma=True)
        p.wait("tensor", "s_wv", p.cnt("s_wv"))
        v_evac_dve = {}
        v_out_cnt = {}
        for tt in range(8):
            vslot = tt % 2
            if tt >= 1:
                p.wait("tensor", "s_dve", v_evac_dve[tt - 1])
            for ec in range(4):
                poff = ec * 512
                for t in range(DT):
                    p.run("tensor", lambda e, t=t, tt=tt, ec=ec, poff=poff: e.matmul(
                        ps[:, poff:poff+512],
                        xn[:, t*T + tt*128: t*T + tt*128 + 128],
                        wv[:, t*D + ec*512: t*D + (ec+1)*512],
                        start=(t == 0), stop=(t == DT-1)),
                        inc="s_pe" if t == DT-1 else None)
            p.wait("vector", "s_pe")
            vsem = "s_vo%d" % vslot
            if tt >= 2:
                p.wait("vector", vsem, v_out_cnt[tt - 2])
            for ec in range(4):
                p.run("vector", lambda e, vslot=vslot, ec=ec: e.tensor_copy(
                    out=vstage[:, vslot*D + ec*512: vslot*D + (ec+1)*512],
                    in_=ps[:, ec*512:(ec+1)*512]), inc="s_dve")
            v_evac_dve[tt] = p.cnt("s_dve")
            p.wait("sync", "s_dve", v_evac_dve[tt])
            p.run("sync", lambda e, tt=tt, vslot=vslot: e.dma_start(
                out=v_o[tt*128:(tt+1)*128, :],
                in_=vstage[:, vslot*D:(vslot+1)*D]), inc=vsem, dma=True)
            v_out_cnt[tt] = p.cnt(vsem)

        for sname in ("s_so0", "s_so1", "s_vo0", "s_vo1"):
            p.wait("gpsimd", sname, p.cnt(sname))
        p.emit(st)
        wv_ctx.__exit__(None, None, None)
    return nc


class Mem:

    def __init__(self, nc, stack, kib, name="arena"):
        self.t = stack.enter_context(
            nc.sbuf_tensor(name, [P, kib * 1024], mybir.dt.uint8))
        self.cap = kib * 1024

    def at(self, byte_off, cols, dtype):
        nb = cols * mybir.dt.size(dtype)
        assert byte_off % 512 == 0 and byte_off + nb <= self.cap, \
            f"bad arena window {byte_off}+{nb} cap {self.cap}"
        return self.t[:, byte_off:byte_off + nb].bitcast(dtype)


def build_launch_b(upto=9, dump=None, p5_tiles=64):
    nc = bass.Bass("TRN2", target_bir_lowering=False)
    dram = lambda n, s, d, k="ExternalInput": nc.dram_tensor(n, s, d, kind=k).ap()
    xT = dram("xT", [D, T], F32)
    qT_i = dram("qT_i", [D, T], BF16)
    kTp = dram("kTp", [16, P, S], BF16)
    vaug = dram("vaug", [16, P, 64*130], BF16)
    Wo_s = dram("Wo_s", [DT, P, DT*P], F32)
    bo_i = dram("bo", [P, DT], F32)
    bv_i = dram("bv", [P, DT], F32)
    suug_i = dram("suug", [P, DT], F32)
    sudp_i = dram("sudp", [P, 64], F32)
    wsu_i = dram("wsu", [P, 64], F32)
    wsg_i = dram("wsg", [P, 64], F32)
    wsd_i = dram("wsd", [P, DT], F32)
    H128_i = dram("H128", [P, P], BF16)
    id64_i = dram("id64", [64, 64], BF16)
    WupT_s = dram("WupT_s", [64, P, DT*P], F32)
    WgT_s = dram("WgT_s", [64, P, DT*P], F32)
    WdT_s = dram("WdT_s", [DT, P, 64*P], F32)
    wq1_s = dram("wq1_s", [DT, P, DT*P], F32)
    wk1_s = dram("wk1_s", [DT, P, DT*P], F32)
    wv1_r = dram("wv1_r", [DT, P, D], F32)
    r2_o = dram("r2_o", [D, T], F32, "ExternalOutput")
    q1T_o = dram("q1T_o", [D, T], BF16, "ExternalOutput")
    k1T_o = dram("k1T_o", [D, T], BF16, "ExternalOutput")
    v1_o = dram("v1_o", [T, D], BF16, "ExternalOutput")
    r1_d = nc.dram_tensor("r1_d", [D, T], F32, kind="Internal").ap()
    dbg_o = nc.dram_tensor("dbg_o", [64*P, T], F32, kind="ExternalOutput").ap() \
        if dump else None

    NBATCH = [3]*21 + [1]
    K = 1024

    with ExitStack() as st:
        E = st.enter_context
        p = Prog(nc)
        for sname in ("s_ld", "s_kv0", "s_kv1", "s_w0", "s_w1", "s_w2", "s_w3",
                      "s_wv", "s_rd0", "s_rd1", "s_pe", "s_act", "s_dve",
                      "s_st0", "s_st1"):
            p.sem(sname, st)

        rows = E(nc.sbuf_tensor("rows", [1, 3072], F32))
        ones_col = E(nc.sbuf_tensor("ones_col", [P, 1], BF16))
        ones_row = E(nc.sbuf_tensor("ones_row", [1, P], F32))
        ones_blk = E(nc.sbuf_tensor("ones_blk", [P, 64], F32))
        id64 = E(nc.sbuf_tensor("id64_s", [64, 64], BF16))
        H128 = E(nc.sbuf_tensor("H128s", [P, P], BF16))
        bo = E(nc.sbuf_tensor("bo_s", [P, DT], F32))
        bv = E(nc.sbuf_tensor("bv_s", [P, DT], F32))
        bvb = E(nc.sbuf_tensor("bvb_s", [P, DT], BF16))
        cb = E(nc.sbuf_tensor("cb_s", [P, DT], F32))
        suug = E(nc.sbuf_tensor("suug_s", [P, DT], F32))
        sudp = E(nc.sbuf_tensor("sudp_s", [P, 64], F32))
        usc = E(nc.sbuf_tensor("usc_s", [P, 64], F32))
        gsc = E(nc.sbuf_tensor("gsc_s", [P, 64], F32))
        dsc = E(nc.sbuf_tensor("dsc_s", [P, DT], F32))
        ps = E(nc.psum_tensor("ps8", [P, 4096], F32))
        M = Mem(nc, st, 184)

        def ld(dst, src):
            p.run("sync", lambda e, dst=dst, src=src: e.dma_start(
                out=dst, in_=src), inc="s_ld", dma=True)
        ld(bo[:], bo_i)
        ld(bv[:], bv_i)
        ld(suug[:], suug_i)
        ld(sudp[:], sudp_i)
        ld(usc[:], wsu_i)
        ld(gsc[:], wsg_i)
        ld(dsc[:], wsd_i)
        ld(H128[:], H128_i)
        ld(id64[:], id64_i)
        p.run("vector", lambda e: e.memset(ones_col[:], 1.0), inc="s_dve")
        p.run("vector", lambda e: e.memset(ones_row[:], 1.0), inc="s_dve")
        p.run("vector", lambda e: e.memset(ones_blk[:], 1.0), inc="s_dve")
        p.wait("vector", "s_ld", p.cnt("s_ld"))
        p.run("vector", lambda e: e.tensor_scalar(
            out=usc[:], in0=usc[:], scalar1=C_UG, scalar2=None, op0=ALU.mult),
            inc="s_dve")
        p.selfsync("vector")
        p.run("vector", lambda e: e.tensor_tensor(
            out=usc[:], in0=usc[:], in1=sudp[:], op=ALU.mult), inc="s_dve")
        p.run("vector", lambda e: e.tensor_scalar(
            out=gsc[:], in0=gsc[:], scalar1=C_UG, scalar2=None, op0=ALU.mult),
            inc="s_dve")
        p.run("vector", lambda e: e.tensor_scalar(
            out=dsc[:], in0=dsc[:], scalar1=C_DP, scalar2=None, op0=ALU.mult),
            inc="s_dve")
        p.run("vector", lambda e: e.tensor_copy(out=bvb[:], in_=bv[:]),
              inc="s_dve")
        const_done = p.cnt("s_dve")

        attnT = M.at(0, DT*T, BF16)
        kbuf = M.at(32*K, 2*S, BF16)
        vbuf = M.at(64*K, 2*64*130, BF16)
        ptb = M.at(98*K, 2*2*3*512, BF16)
        qbuf = M.at(110*K, 2*1024, BF16)
        tmp = M.at(114*K, 1024, BF16)
        mvin = M.at(116*K, 512, BF16)
        rec = M.at(118*K, 1024, F32)
        silu = M.at(162*K, 4*512, BF16)
        rch = M.at(176*K, 2*512, F32)
        stg = M.at(180*K, 2*512, F32)
        vstage = M.at(104*K, 2*D, BF16)
        stgb = M.at(112*K, 2*512, BF16)
        r1 = M.at(32*K, DT*T, F32)
        wstrip = M.at(96*K, 2*DT*P, BF16)
        xbf1 = M.at(96*K, DT*T, BF16)
        xn1f = M.at(104*K, DT*T, F32)
        x1 = M.at(0, DT*T, BF16)
        fscr = M.at(96*K, T, F32)
        hid = M.at(32*K, 64*T, BF16)
        wug = M.at(168*K, 4*DT*P, BF16)
        hscr = M.at(160*K, T, BF16)
        dbuf = M.at(0, 2*64*P, BF16)
        xn2 = M.at(0, DT*T, BF16)
        r2 = M.at(32*K, DT*T, F32)
        xbf2 = M.at(96*K, DT*T, BF16)
        wv1 = M.at(32*K, DT*D, BF16)
        wqk = M.at(96*K, 2*DT*P, BF16)

        def dump_tiles(tiles):
            p.wait("gpsimd", "s_dve", p.cnt("s_dve"))
            p.wait("gpsimd", "s_act", p.cnt("s_act"))
            p.wait("gpsimd", "s_pe", p.cnt("s_pe"))
            for i, ap in enumerate(tiles):
                p.run("gpsimd", lambda e, i=i, ap=ap: e.dma_start(
                    out=dbg_o[i*P:(i+1)*P, :], in_=ap), inc="s_st0", dma=True)

        def finish():
            for sname in ("s_st0", "s_st1"):
                p.wait("gpsimd", sname, p.cnt(sname))
            p.emit(st)

        p.wait("tensor", "s_dve", const_done)
        kv_used_pe = {}
        q_used_pe = {}
        norm_done_dve = 0
        for hp in range(16):
            slot = hp % 2
            kvsem = "s_kv%d" % slot
            if hp >= 2:
                p.wait("sync", "s_pe", kv_used_pe[hp - 2])
            p.run("sync", lambda e, hp=hp, slot=slot: e.dma_start(
                out=kbuf[:, slot*S:(slot+1)*S], in_=kTp[hp]), inc=kvsem, dma=True)
            p.run("sync", lambda e, hp=hp, slot=slot: e.dma_start(
                out=vbuf[:, slot*64*130:(slot+1)*64*130], in_=vaug[hp]),
                inc=kvsem, dma=True)
            p.run("sync", lambda e, hp=hp, slot=slot: e.dma_start(
                out=qbuf[:, slot*1024:(slot+1)*1024], in_=qT_i[hp*P:(hp+1)*P, :]),
                inc=kvsem, dma=True)
            p.wait("tensor", kvsem, p.cnt(kvsem))
            for ch in range(2):
                qsl = slice(slot*1024 + ch*512, slot*1024 + (ch+1)*512)
                asl = slice(hp*T + ch*512, hp*T + (ch+1)*512)
                exp_done = {}
                av_done = {}
                batch_of = []
                kt0 = 0
                for bs in NBATCH:
                    batch_of.append((kt0, bs))
                    kt0 += bs

                def emit_scores(b, hh):
                    kt0, bs = batch_of[b]
                    base = slice(hh*64, (hh+1)*64)
                    if b >= 1:
                        p.wait("tensor", "s_act", exp_done.get((b-1, hh), 0))
                    for j in range(bs):
                        kt = kt0 + j
                        bank = hh*3 + j
                        p.run("tensor",
                              lambda e, base=base, kt=kt, bank=bank, slot=slot, qsl=qsl: e.matmul(
                                  ps[:, bank*512:(bank+1)*512],
                                  kbuf[base, slot*S + kt*P: slot*S + (kt+1)*P],
                                  qbuf[base, qsl],
                                  start=True, stop=True),
                              inc="s_pe" if j == bs-1 else None)
                    sc_done[(b, hh)] = p.cnt("s_pe")

                def emit_exp(b, hh):
                    kt0, bs = batch_of[b]
                    par = b % 2
                    po = par*2*3*512 + hh*3*512
                    p.wait("scalar", "s_pe", sc_done[(b, hh)])
                    if b >= 2:
                        p.wait("scalar", "s_pe", av_done.get((b-2, hh), 0))
                    p.run("scalar", lambda e, po=po, hh=hh, bs=bs: e.activation(
                        out=ptb[:, po:po + bs*512],
                        in_=ps[:, hh*3*512: hh*3*512 + bs*512],
                        func=AF.Exp, scale=SC0), inc="s_act")
                    exp_done[(b, hh)] = p.cnt("s_act")

                def emit_av(b, hh):
                    kt0, bs = batch_of[b]
                    par = b % 2
                    fbank = 6 + hh
                    p.wait("tensor", "s_act", exp_done[(b, hh)])
                    for j in range(bs):
                        kt = kt0 + j
                        po = par*2*3*512 + hh*3*512 + j*512
                        p.run("tensor",
                              lambda e, kt=kt, po=po, slot=slot, fbank=fbank, hh=hh: e.matmul(
                                  ps[0:65, fbank*512:(fbank+1)*512],
                                  vbuf[:, slot*64*130 + kt*130 + hh*65: slot*64*130 + kt*130 + hh*65 + 65],
                                  ptb[:, po:po+512],
                                  start=(kt == 0), stop=(kt == 63)),
                              inc="s_pe" if j == bs-1 else None)
                    av_done[(b, hh)] = p.cnt("s_pe")

                sc_done = {}
                nb = len(NBATCH)
                for b in range(nb):
                    emit_scores(b, 0)
                    emit_exp(b, 0)
                    emit_scores(b, 1)
                    emit_exp(b, 1)
                    if b == 0:
                        p.wait("tensor", "s_dve", norm_done_dve)
                    if b >= 1:
                        emit_av(b-1, 0)
                        emit_av(b-1, 1)
                emit_av(nb-1, 0)
                emit_av(nb-1, 1)
                kv_used_pe[hp] = p.cnt("s_pe")
                p.wait("vector", "s_pe", av_done[(len(NBATCH)-1, 1)])
                p.selfsync("vector")
                p.run("vector", lambda e: e.reciprocal(
                    rec[64:65, 0:512], ps[64:65, 6*512:6*512+512]), inc="s_dve")
                p.run("vector", lambda e: e.reciprocal(
                    rec[64:65, 512:1024], ps[64:65, 7*512:7*512+512]), inc="s_dve")
                p.run("vector", lambda e: e.tensor_copy(
                    out=tmp[0:64, 0:512], in_=ps[0:64, 6*512:6*512+512]),
                    inc="s_dve")
                p.run("vector", lambda e: e.tensor_copy(
                    out=tmp[0:64, 512:1024], in_=ps[0:64, 7*512:7*512+512]),
                    inc="s_dve")
                rec_done = p.cnt("s_dve")
                p.wait("tensor", "s_dve", rec_done)
                p.run("tensor", lambda e: e.matmul(
                    ps[0:64, 6*512:6*512+512], ones_blk[64:65, 0:64],
                    rec[64:65, 0:512], start=True, stop=True), inc="s_pe")
                p.run("tensor", lambda e: e.matmul(
                    ps[0:64, 7*512:7*512+512], ones_blk[64:65, 0:64],
                    rec[64:65, 512:1024], start=True, stop=True), inc="s_pe")
                bc_done = p.cnt("s_pe")
                p.wait("vector", "s_pe", bc_done)
                p.run("vector", lambda e, asl=asl: e.tensor_tensor(
                    out=attnT[0:64, asl], in0=tmp[0:64, 0:512],
                    in1=ps[0:64, 6*512:6*512+512], op=ALU.mult), inc="s_dve")
                p.run("vector", lambda e: e.tensor_tensor(
                    out=mvin[0:64, 0:512], in0=tmp[0:64, 512:1024],
                    in1=ps[0:64, 7*512:7*512+512], op=ALU.mult), inc="s_dve")
                mv_done = p.cnt("s_dve")
                p.wait("tensor", "s_dve", mv_done)
                p.run("tensor", lambda e: e.matmul(
                    ps[64:128, 7*512:7*512+512], id64[0:64, 0:64],
                    mvin[0:64, 0:512],
                    start=True, stop=True, tile_position=(0, 64)), inc="s_pe")
                mvmm = p.cnt("s_pe")
                p.wait("vector", "s_pe", mvmm)
                p.run("vector", lambda e, asl=asl: e.tensor_copy(
                    out=attnT[64:128, asl], in_=ps[64:128, 7*512:7*512+512]),
                    inc="s_dve")
                norm_done_dve = p.cnt("s_dve")
        att_done_dve = norm_done_dve
        att_done_pe = p.cnt("s_pe")
        if upto <= 1:
            if dump == "attnT":
                dump_tiles([attnT[:, t*T:(t+1)*T] for t in range(DT)])
            finish()
            return nc

        r1_tiles = [r1[:, t*T:(t+1)*T] for t in range(DT)]
        p.wait("sync", "s_pe", att_done_pe)
        p.wait("gpsimd", "s_dve", att_done_dve)
        for E_ in range(DT):
            p.run("sync", lambda e, E_=E_: e.dma_start(
                out=r1_tiles[E_], in_=xT[E_*P:(E_+1)*P, :]), inc="s_ld", dma=True)
        xt_ld = p.cnt("s_ld")
        p.wait("tensor", "s_ld", xt_ld)
        p.wait("vector", "s_ld", xt_ld)
        evac_dve = {}
        cb_done = {}
        for E_ in range(DT):
            slot = E_ % 2
            wsem = "s_w%d" % slot
            if E_ >= 2:
                p.wait("gpsimd", "s_dve", evac_dve[E_ - 2])
            p.run("gpsimd", lambda e, E_=E_, slot=slot: e.dma_start(
                out=wstrip[:, slot*DT*P:(slot+1)*DT*P], in_=Wo_s[E_]),
                inc=wsem, dma=True)
            p.wait("tensor", wsem, p.cnt(wsem))
            if E_ == 0:
                p.wait("tensor", "s_dve", att_done_dve)
            if E_ >= 1:
                p.wait("tensor", "s_dve", cb_done[E_ - 1])
            for t in range(DT):
                p.run("tensor", lambda e, slot=slot, t=t: e.matmul(
                    ps[:, 0:1],
                    wstrip[:, slot*DT*P + t*P: slot*DT*P + (t+1)*P],
                    bvb[:, t:t+1], start=(t == 0), stop=(t == DT-1)),
                    inc="s_pe" if t == DT-1 else None)
            p.wait("vector", "s_pe")
            p.run("vector", lambda e, E_=E_: e.tensor_scalar(
                out=cb[:, E_:E_+1], in0=ps[:, 0:1],
                scalar1=bo[:, E_:E_+1], scalar2=None, op0=ALU.add), inc="s_dve")
            cb_done[E_] = p.cnt("s_dve")
            if E_ >= 2:
                p.wait("tensor", "s_dve", evac_dve[E_ - 2])
            for ch in range(2):
                bank = 4 + (E_ % 2)*2 + ch
                for t in range(DT):
                    p.run("tensor", lambda e, slot=slot, t=t, ch=ch, bank=bank: e.matmul(
                        ps[:, bank*512:(bank+1)*512],
                        wstrip[:, slot*DT*P + t*P: slot*DT*P + (t+1)*P],
                        attnT[:, t*T + ch*512: t*T + (ch+1)*512],
                        start=(t == 0), stop=(t == DT-1)),
                        inc="s_pe" if t == DT-1 else None)
            p.wait("vector", "s_pe")
            p.wait("vector", "s_dve", cb_done[E_])
            for ch in range(2):
                bank = 4 + (E_ % 2)*2 + ch
                sl = slice(E_*T + ch*512, E_*T + (ch+1)*512)
                p.run("vector", lambda e, E_=E_, sl=sl, bank=bank: e.scalar_tensor_tensor(
                    out=r1[:, sl], in0=ps[:, bank*512:(bank+1)*512],
                    scalar=cb[:, E_:E_+1], in1=r1[:, sl],
                    op0=ALU.add, op1=ALU.add), inc="s_dve")
            evac_dve[E_] = p.cnt("s_dve")
            p.wait("sync", "s_dve", evac_dve[E_])
            p.run("sync", lambda e, E_=E_: e.dma_start(
                out=r1_d[E_*P:(E_+1)*P, :], in_=r1[:, E_*T:(E_+1)*T]),
                inc="s_st0", dma=True)
        wo_done_dve = p.cnt("s_dve")
        r1_stored = p.cnt("s_st0")
        if upto <= 2:
            if dump == "r1":
                dump_tiles(r1_tiles)
            finish()
            return nc

        p.wait("vector", "s_pe", p.cnt("s_pe"))
        p.wait("tensor", "s_dve", wo_done_dve)
        emit_ln(p, r1_tiles, None, xbf1, ps, rows, ones_col, ones_row,
                su_ptr=suug, dst32=xn1f)
        xn1_done = p.cnt("s_dve")
        if upto <= 3:
            if dump == "xn1f":
                dump_tiles([xn1f[:, t*T:(t+1)*T] for t in range(DT)])
            finish()
            return nc

        slots = {t: xn1f[:, t*T:(t+1)*T] for t in range(DT)}
        free_slot = fscr[:, 0:T]
        p.wait("vector", "s_dve", xn1_done)
        for sbit in (1, 2, 4, 8):
            last = (sbit == 8)
            for a in range(DT):
                if a & sbit:
                    continue
                b_ = a | sbit
                Aap, Bap = slots[a], slots[b_]
                p.selfsync("vector")
                if not last:
                    newA = free_slot
                    p.run("vector", lambda e, newA=newA, Aap=Aap, Bap=Bap: e.tensor_tensor(
                        out=newA, in0=Aap, in1=Bap, op=ALU.add), inc="s_dve")
                    p.run("vector", lambda e, Aap=Aap, Bap=Bap: e.tensor_tensor(
                        out=Bap, in0=Aap, in1=Bap, op=ALU.subtract), inc="s_dve")
                    free_slot = Aap
                    slots[a] = newA
                else:
                    p.run("vector", lambda e, a=a, Aap=Aap, Bap=Bap: e.tensor_tensor(
                        out=x1[:, a*T:(a+1)*T], in0=Aap, in1=Bap, op=ALU.add),
                        inc="s_dve")
                    p.run("vector", lambda e, b_=b_, Aap=Aap, Bap=Bap: e.tensor_tensor(
                        out=x1[:, b_*T:(b_+1)*T], in0=Aap, in1=Bap, op=ALU.subtract),
                        inc="s_dve")
        h16_done = p.cnt("s_dve")
        p.wait("tensor", "s_dve", h16_done)
        h128_ev = {}
        for t in range(DT):
            for ch in range(2):
                g = t*2 + ch
                bank = g % 2
                eng, esem = ("scalar", "s_act") if (g % 2) == 0 else ("vector", "s_dve")
                if g >= 2:
                    p.wait("tensor", h128_ev[g-2][1], h128_ev[g-2][0])
                p.run("tensor", lambda e, t=t, ch=ch, bank=bank: e.matmul(
                    ps[:, bank*512:(bank+1)*512], H128[:, :],
                    x1[:, t*T + ch*512: t*T + (ch+1)*512],
                    start=True, stop=True), inc="s_pe")
                p.wait(eng, "s_pe", p.cnt("s_pe"))
                if eng == "scalar":
                    p.run("scalar", lambda e, t=t, ch=ch, bank=bank: e.activation(
                        out=x1[:, t*T + ch*512: t*T + (ch+1)*512],
                        in_=ps[:, bank*512:(bank+1)*512], func=AF.Identity),
                        inc="s_act")
                else:
                    p.run("vector", lambda e, t=t, ch=ch, bank=bank: e.tensor_copy(
                        out=x1[:, t*T + ch*512: t*T + (ch+1)*512],
                        in_=ps[:, bank*512:(bank+1)*512]), inc="s_dve")
                h128_ev[g] = (p.cnt(esem), esem)
        x1_done_act = p.cnt("s_act")
        x1_done_dve = p.cnt("s_dve")
        if upto <= 4:
            if dump == "x1":
                dump_tiles([x1[:, t*T:(t+1)*T] for t in range(DT)])
            finish()
            return nc

        p.wait("tensor", "s_act", x1_done_act)
        p.wait("tensor", "s_dve", x1_done_dve)
        p.wait("vector", "s_st0", r1_stored)
        p.wait("gpsimd", "s_dve", wo_done_dve)
        ug_stt = {}
        ug_silu = {}
        wug_used = {}
        SBITS = (1, 2, 4, 8, 16, 32)
        level = {}
        done_pairs = set()

        def try_butterflies():
            progressed = True
            while progressed:
                progressed = False
                for s_idx, sbit in enumerate(SBITS):
                    for a in range(64):
                        if (a & sbit) or ((s_idx, a) in done_pairs):
                            continue
                        b_ = a | sbit
                        if level.get(a, -1) == s_idx and level.get(b_, -1) == s_idx:
                            Aap = hid[:, a*T:(a+1)*T]
                            Bap = hid[:, b_*T:(b_+1)*T]
                            p.selfsync("vector")
                            p.run("vector", lambda e, Aap=Aap, Bap=Bap: e.tensor_tensor(
                                out=Aap, in0=Aap, in1=Bap, op=ALU.add), inc="s_dve")
                            p.selfsync("vector")
                            p.run("vector", lambda e, Aap=Aap, Bap=Bap: e.scalar_tensor_tensor(
                                out=Bap, in0=Bap, scalar=-2.0, in1=Aap,
                                op0=ALU.mult, op1=ALU.add), inc="s_dve")
                            done_pairs.add((s_idx, a))
                            level[a] = level[b_] = s_idx + 1
                            progressed = True

        for E_ in range(p5_tiles):
            for kind in range(2):
                si = E_*2 + kind
                slot = si % 4
                wsem = "s_w%d" % slot
                if si >= 4:
                    p.wait("gpsimd", "s_pe", wug_used[si - 4])
                src = WupT_s if kind == 0 else WgT_s
                p.run("gpsimd", lambda e, src=src, E_=E_, slot=slot: e.dma_start(
                    out=wug[:, slot*DT*P:(slot+1)*DT*P], in_=src[E_]),
                    inc=wsem, dma=True)
                p.wait("tensor", wsem, p.cnt(wsem))
            for ch in range(2):
                gg = E_*2 + ch
                g = gg % 4
                if gg >= 4:
                    p.wait("tensor", "s_dve", ug_stt[gg - 4])
                    p.wait("tensor", "s_act", ug_silu[gg - 4])
                for kind in range(2):
                    slot = (E_*2 + kind) % 4
                    bank = g*2 + kind
                    for t in range(DT):
                        p.run("tensor", lambda e, slot=slot, t=t, ch=ch, bank=bank: e.matmul(
                            ps[:, bank*512:(bank+1)*512],
                            wug[:, slot*DT*P + t*P: slot*DT*P + (t+1)*P],
                            x1[:, t*T + ch*512: t*T + (ch+1)*512],
                            start=(t == 0), stop=(t == DT-1)),
                            inc="s_pe" if t == DT-1 else None)
                wug_used[E_*2 + 1] = wug_used[E_*2] = p.cnt("s_pe")
                p.wait("scalar", "s_pe", p.cnt("s_pe"))
                sslot = gg % 4
                p.run("scalar", lambda e, g=g, E_=E_, sslot=sslot: e.activation(
                    out=silu[:, sslot*512:(sslot+1)*512],
                    in_=ps[:, (g*2+1)*512:(g*2+2)*512],
                    func=AF.Silu, scale=gsc[:, E_:E_+1]), inc="s_act")
                ug_silu[gg] = p.cnt("s_act")
                p.wait("vector", "s_act", ug_silu[gg])
                hsl = slice(E_*T + ch*512, E_*T + (ch+1)*512)
                p.run("vector", lambda e, g=g, E_=E_, hsl=hsl, sslot=sslot: e.scalar_tensor_tensor(
                    out=hid[:, hsl], in0=ps[:, (g*2)*512:(g*2+1)*512],
                    scalar=usc[:, E_:E_+1], in1=silu[:, sslot*512:(sslot+1)*512],
                    op0=ALU.mult, op1=ALU.mult), inc="s_dve")
                ug_stt[gg] = p.cnt("s_dve")
            level[E_] = 0
            try_butterflies()
        ug_done = p.cnt("s_dve")
        ug_pe_done = p.cnt("s_pe")
        if upto <= 5:
            finish()
            return nc

        hslots = {t: hid[:, t*T:(t+1)*T] for t in range(64)}
        assert all(level[t] == len(SBITS) for t in range(64)), "wave incomplete"
        h64_done = p.cnt("s_dve")
        p.wait("tensor", "s_dve", h64_done)
        h128b_ev = {}
        for t in range(64):
            for ch in range(2):
                g = t*2 + ch
                bank = g % 2
                eng, esem = ("scalar", "s_act") if (g % 2) == 0 else ("vector", "s_dve")
                if g >= 2:
                    p.wait("tensor", h128b_ev[g-2][1], h128b_ev[g-2][0])
                p.run("tensor", lambda e, t=t, ch=ch, bank=bank: e.matmul(
                    ps[:, bank*512:(bank+1)*512], H128[:, :],
                    hid[:, t*T + ch*512: t*T + (ch+1)*512],
                    start=True, stop=True), inc="s_pe")
                p.wait(eng, "s_pe", p.cnt("s_pe"))
                if eng == "scalar":
                    p.run("scalar", lambda e, t=t, ch=ch, bank=bank: e.activation(
                        out=hid[:, t*T + ch*512: t*T + (ch+1)*512],
                        in_=ps[:, bank*512:(bank+1)*512], func=AF.Identity),
                        inc="s_act")
                else:
                    p.run("vector", lambda e, t=t, ch=ch, bank=bank: e.tensor_copy(
                        out=hid[:, t*T + ch*512: t*T + (ch+1)*512],
                        in_=ps[:, bank*512:(bank+1)*512]), inc="s_dve")
                h128b_ev[g] = (p.cnt(esem), esem)
        x2_done_act = p.cnt("s_act")
        x2_done_dve = p.cnt("s_dve")
        if upto <= 6:
            if dump == "x2":
                dump_tiles([hslots[t] for t in range(64)])
            finish()
            return nc

        p.wait("gpsimd", "s_pe", ug_pe_done)
        p.wait("sync", "s_st0", r1_stored)
        p.wait("sync", "s_dve", ug_done)
        p.wait("tensor", "s_act", x2_done_act)
        p.wait("tensor", "s_dve", x2_done_dve)
        dn_stt = {}
        dbuf_used = {}
        for E_ in range(DT):
            slot = E_ % 2
            wsem = "s_w%d" % slot
            if E_ >= 2:
                p.wait("gpsimd", "s_pe", dbuf_used[E_ - 2])
            p.run("gpsimd", lambda e, E_=E_, slot=slot: e.dma_start(
                out=dbuf[:, slot*64*P:(slot+1)*64*P], in_=WdT_s[E_]),
                inc=wsem, dma=True)
            p.wait("tensor", wsem, p.cnt(wsem))
            if E_ >= 2:
                p.wait("tensor", "s_dve", dn_stt[E_ - 2])
            for ch in range(2):
                bank = 4 + (E_ % 2)*2 + (ch % 2)
                rdsem = "s_rd%d" % (ch % 2)
                p.wait("sync", "s_dve", dn_stt.get(E_ - 1, 0))
                p.run("sync", lambda e, E_=E_, ch=ch: e.dma_start(
                    out=rch[:, (ch % 2)*512:(ch % 2 + 1)*512],
                    in_=r1_d[E_*P:(E_+1)*P, ch*512:(ch+1)*512]),
                    inc=rdsem, dma=True)
                for t in range(64):
                    p.run("tensor", lambda e, slot=slot, t=t, ch=ch, bank=bank: e.matmul(
                        ps[:, bank*512:(bank+1)*512],
                        dbuf[:, slot*64*P + t*P: slot*64*P + (t+1)*P],
                        hslots[t][:, ch*512:(ch+1)*512],
                        start=(t == 0), stop=(t == 63)),
                        inc="s_pe" if t == 63 else None)
                dbuf_used[E_] = p.cnt("s_pe")
                p.wait("vector", "s_pe")
                p.wait("vector", rdsem, p.cnt(rdsem))
                stsem = "s_st%d" % (ch % 2)
                p.wait("vector", stsem, p.cnt(stsem))
                p.run("vector", lambda e, E_=E_, ch=ch, bank=bank: e.scalar_tensor_tensor(
                    out=stg[:, (ch % 2)*512:(ch % 2+1)*512],
                    in0=ps[:, bank*512:(bank+1)*512],
                    scalar=dsc[:, E_:E_+1], in1=rch[:, (ch % 2)*512:(ch % 2+1)*512],
                    op0=ALU.mult, op1=ALU.add), inc="s_dve")
                dn_stt[E_] = p.cnt("s_dve")
                p.wait("sync", "s_dve", dn_stt[E_])
                p.run("sync", lambda e, E_=E_, ch=ch: e.dma_start(
                    out=r2_o[E_*P:(E_+1)*P, ch*512:(ch+1)*512],
                    in_=stg[:, (ch % 2)*512:(ch % 2+1)*512]),
                    inc=stsem, dma=True)
        dn_done = p.cnt("s_dve")
        r2_st0, r2_st1 = p.cnt("s_st0"), p.cnt("s_st1")
        if upto <= 7:
            finish()
            return nc

        p.wait("sync", "s_pe", p.cnt("s_pe"))
        p.wait("sync", "s_st0", r2_st0)
        p.wait("sync", "s_st1", r2_st1)
        r2_tiles = [r2[:, t*T:(t+1)*T] for t in range(DT)]
        for t in range(DT):
            ld(r2_tiles[t], r2_o[t*P:(t+1)*P, :])
        p.wait("vector", "s_ld", p.cnt("s_ld"))
        p.wait("vector", "s_pe", p.cnt("s_pe"))
        p.wait("tensor", "s_dve", dn_done)
        emit_ln(p, r2_tiles, xn2, xbf2, ps, rows, ones_col, ones_row)
        xn2_done = p.cnt("s_dve")
        p.wait("tensor", "s_dve", xn2_done)
        if upto <= 8:
            finish()
            return nc

        strip_plan = [("q", e_) for e_ in range(DT)] + [("k", e_) for e_ in range(DT)]
        strip_done_dve = {}
        stage_out_cnt = {}
        group_evac_dve = {}
        for si, (kind, E_) in enumerate(strip_plan):
            slot = si % 2
            wsem = "s_w%d" % slot
            src = wq1_s if kind == "q" else wk1_s
            dst = q1T_o if kind == "q" else k1T_o
            if si >= 2:
                p.wait("gpsimd", "s_dve", strip_done_dve[si - 2])
            else:
                p.wait("gpsimd", "s_dve", xn2_done)
            p.run("gpsimd", lambda e, src=src, E_=E_, slot=slot: e.dma_start(
                out=wqk[:, slot*DT*P:(slot+1)*DT*P], in_=src[E_]),
                inc=wsem, dma=True)
            p.wait("tensor", wsem, p.cnt(wsem))
            for ch in range(2):
                g = si * 2 + ch
                bank = 4 + (g % 2)
                poff = bank * 512
                if g >= 2:
                    p.wait("tensor", "s_dve", group_evac_dve[g - 2])
                for t in range(DT):
                    p.run("tensor", lambda e, slot=slot, t=t, ch=ch, poff=poff: e.matmul(
                        ps[:, poff:poff+512],
                        wqk[:, slot*DT*P + t*P: slot*DT*P + (t+1)*P],
                        xn2[:, t*T + ch*512: t*T + (ch+1)*512],
                        start=(t == 0), stop=(t == DT-1)),
                        inc="s_pe" if t == DT-1 else None)
                p.wait("vector", "s_pe")
                sslot = g % 2
                osem = "s_st%d" % sslot
                if g >= 2:
                    p.wait("vector", osem, stage_out_cnt[g - 2])
                p.run("vector", lambda e, poff=poff, sslot=sslot: e.tensor_copy(
                    out=stgb[:, sslot*512:(sslot+1)*512],
                    in_=ps[:, poff:poff+512]), inc="s_dve")
                group_evac_dve[g] = p.cnt("s_dve")
                p.wait("sync", "s_dve", group_evac_dve[g])
                p.run("sync", lambda e, dst=dst, E_=E_, ch=ch, sslot=sslot: e.dma_start(
                    out=dst[E_*P:(E_+1)*P, ch*512:(ch+1)*512],
                    in_=stgb[:, sslot*512:(sslot+1)*512]),
                    inc=osem, dma=True)
                stage_out_cnt[g] = p.cnt(osem)
            strip_done_dve[si] = p.cnt("s_dve")
        qk1_done_pe = p.cnt("s_pe")

        p.wait("gpsimd", "s_dve", xn2_done)
        for t in range(DT):
            p.run("gpsimd", lambda e, t=t: e.dma_start(
                out=wv1[:, t*D:(t+1)*D], in_=wv1_r[t]), inc="s_wv", dma=True)
        p.wait("tensor", "s_wv", p.cnt("s_wv"))
        v_evac_dve = {}
        v_out_cnt = {}
        for tt in range(8):
            vslot = tt % 2
            if tt >= 1:
                p.wait("tensor", "s_dve", v_evac_dve[tt - 1])
            for ec in range(4):
                poff = ec * 512
                for t in range(DT):
                    p.run("tensor", lambda e, t=t, tt=tt, ec=ec, poff=poff: e.matmul(
                        ps[:, poff:poff+512],
                        xn2[:, t*T + tt*128: t*T + tt*128 + 128],
                        wv1[:, t*D + ec*512: t*D + (ec+1)*512],
                        start=(t == 0), stop=(t == DT-1)),
                        inc="s_pe" if t == DT-1 else None)
            p.wait("vector", "s_pe")
            vsem = "s_st%d" % vslot
            if tt >= 2:
                p.wait("vector", vsem, v_out_cnt[tt - 2])
            for ec in range(4):
                p.run("vector", lambda e, vslot=vslot, ec=ec: e.tensor_copy(
                    out=vstage[:, vslot*D + ec*512: vslot*D + (ec+1)*512],
                    in_=ps[:, ec*512:(ec+1)*512]), inc="s_dve")
            v_evac_dve[tt] = p.cnt("s_dve")
            p.wait("sync", "s_dve", v_evac_dve[tt])
            p.run("sync", lambda e, tt=tt, vslot=vslot: e.dma_start(
                out=v1_o[tt*128:(tt+1)*128, :],
                in_=vstage[:, vslot*D:(vslot+1)*D]), inc=vsem, dma=True)
            v_out_cnt[tt] = p.cnt(vsem)

        for sname in ("s_st0", "s_st1"):
            p.wait("gpsimd", sname, p.cnt(sname))
        p.emit(st)
    return nc



def col_strips(w, tin, tout):
    return np.ascontiguousarray(
        w.reshape(tin, P, tout, P).transpose(2, 1, 0, 3).reshape(tout, P, tin*P))


def row_strips(w, tin):
    return np.ascontiguousarray(np.asarray(w).reshape(tin, P, -1))


def vec_tiles(v):
    return np.ascontiguousarray(np.asarray(v).reshape(-1, P).T)


def build_launch_c(upto=9, sum_pack=1):
    nc = bass.Bass("TRN2", target_bir_lowering=False)
    dram = lambda n, s, d, k="ExternalInput": nc.dram_tensor(n, s, d, kind=k).ap()
    r2T = dram("r2T", [D, T], F32)
    q1T_i = dram("q1T_i", [D, T], BF16)
    k1Tp = dram("k1Tp", [16, P, S], BF16)
    v1p = dram("v1p", [16, P, 64*128], BF16)
    tri_i = dram("tri", [P, 8*P], BF16)
    wout_s = dram("wout_s", [DT, P, DT*P], F32)
    wqg_s = dram("wqg_s", [HT1, P, DT*P], I32)
    wqu_s = dram("wqu_s", [HT1, P, DT*P], I32)
    wqd_s = dram("wqd_s", [DT, P, HT1*P], I32)
    sg_i = dram("sg", [P, HT1], F32)
    su2_i = dram("su2", [P, HT1], F32)
    sd_i = dram("sd", [P, DT], F32)
    yT_o = dram("yT_o", [D, T], F32, "ExternalOutput")
    r3_d = nc.dram_tensor("r3_d", [D, T], F32, kind="Internal").ap()

    K = 1024
    with ExitStack() as st:
        E = st.enter_context
        p = Prog(nc)
        for sname in ("s_ld", "s_kv0", "s_kv1", "s_w0", "s_w1", "s_w2", "s_w3",
                      "s_rd0", "s_rd1", "s_pe", "s_act", "s_dve", "s_cast",
                      "s_st0", "s_st1"):
            p.sem(sname, st)

        rows = E(nc.sbuf_tensor("rows", [1, 3072], F32))
        ones_col = E(nc.sbuf_tensor("ones_col", [P, 1], BF16))
        ones_row = E(nc.sbuf_tensor("ones_row", [1, P], F32))
        tri = E(nc.sbuf_tensor("tri_s", [P, 8*P], BF16))
        onesf = E(nc.sbuf_tensor("onesf_s", [P, P], F32))
        recb = E(nc.sbuf_tensor("recb_s", [P, 512], F32))
        sg = E(nc.sbuf_tensor("sg_s", [P, HT1], F32))
        su2 = E(nc.sbuf_tensor("su2_s", [P, HT1], F32))
        sd = E(nc.sbuf_tensor("sd_s", [P, DT], F32))
        ps = E(nc.psum_tensor("ps8", [P, 4096], F32))
        M = Mem(nc, st, 184)

        attnT = M.at(0, DT*T, BF16)
        kbuf = M.at(32*K, 2*S, BF16)
        vbuf = M.at(64*K, 2*64*P, BF16)
        ptb = M.at(96*K, 2*3*512, BF16)
        qbuf = M.at(104*K, 2*1024, BF16)
        tmp = M.at(108*K, 512, BF16)
        rec = M.at(110*K, 512, F32)
        r3 = M.at(32*K, DT*T, F32)
        wstrip = M.at(96*K, 2*DT*P, BF16)
        xbf3 = M.at(96*K, DT*T, BF16)
        xn3 = M.at(0, DT*T, BF16)
        gu = M.at(32*K, HT1*T, BF16)
        wug4 = M.at(128*K, 4*DT*P, BF16)
        silu = M.at(144*K, 4*512, BF16)
        dbuf = M.at(120*K, 2*HT1*P, BF16)
        rch = M.at(166*K, 2*512, F32)
        stg = M.at(170*K, 2*512, F32)

        def ld(dst, src):
            p.run("sync", lambda e, dst=dst, src=src: e.dma_start(
                out=dst, in_=src), inc="s_ld", dma=True)
        ld(tri[:], tri_i)
        ld(sg[:], sg_i)
        ld(su2[:], su2_i)
        ld(sd[:], sd_i)
        p.run("vector", lambda e: e.memset(ones_col[:], 1.0), inc="s_dve")
        p.run("vector", lambda e: e.memset(ones_row[:], 1.0), inc="s_dve")
        p.run("vector", lambda e: e.memset(onesf[:], 1.0), inc="s_dve")
        const_done = p.cnt("s_dve")
        const_ld = p.cnt("s_ld")
        p.wait("tensor", "s_ld", const_ld)
        p.wait("tensor", "s_dve", const_done)
        p.wait("vector", "s_ld", const_ld)

        kv_used_pe = {}
        norm_done_dve = 0
        for h in range(16):
            slot = h % 2
            kvsem = "s_kv%d" % slot
            if h >= 2:
                p.wait("sync", "s_pe", kv_used_pe[h - 2])
            p.run("sync", lambda e, h=h, slot=slot: e.dma_start(
                out=kbuf[:, slot*S:(slot+1)*S], in_=k1Tp[h]), inc=kvsem, dma=True)
            p.run("sync", lambda e, h=h, slot=slot: e.dma_start(
                out=vbuf[:, slot*64*P:(slot+1)*64*P], in_=v1p[h]),
                inc=kvsem, dma=True)
            p.run("sync", lambda e, h=h, slot=slot: e.dma_start(
                out=qbuf[:, slot*1024:(slot+1)*1024], in_=q1T_i[h*P:(h+1)*P, :]),
                inc=kvsem, dma=True)
            p.wait("tensor", kvsem, p.cnt(kvsem))
            for ch in range(2):
                qsl = slice(slot*1024 + ch*512, slot*1024 + (ch+1)*512)
                asl = slice(h*T + ch*512, h*T + (ch+1)*512)
                jmax = min(4*ch + 4, 8)
                kts_full = [(ck, jk) for ck in range(8) for jk in range(min(4*ch, 8))]
                kts_part = [(ck, jk) for ck in range(8) for jk in range(4*ch, jmax)]
                kts = kts_full + kts_part
                nfull = len(kts_full)
                batches = []
                i0 = 0
                while i0 < len(kts):
                    bs = min(3, len(kts) - i0)
                    batches.append(kts[i0:i0+bs])
                    i0 += bs
                exp_done = {}
                av_done = {}
                msk_done = {}
                sc_done = {}
                nkts = len(kts)
                nsub = min(sum_pack, nkts)
                sub_first = {r: min(i for i in range(nkts) if i % nsub == r)
                             for r in range(nsub)}
                sub_last = {r: max(i for i in range(nkts) if i % nsub == r)
                            for r in range(nsub)}

                def emit_scores(b):
                    bk = batches[b]
                    par = b % 2
                    if b >= 2:
                        p.wait("tensor", "s_act", exp_done.get(b-2, 0))
                    for j, (ck, jk) in enumerate(bk):
                        kt = ck*8 + jk
                        bank = par*3 + j
                        p.run("tensor", lambda e, kt=kt, bank=bank, slot=slot, qsl=qsl: e.matmul(
                            ps[:, bank*512:(bank+1)*512],
                            kbuf[:, slot*S + kt*P: slot*S + (kt+1)*P],
                            qbuf[:, qsl], start=True, stop=True),
                            inc="s_pe" if j == len(bk)-1 else None)
                    sc_done[b] = p.cnt("s_pe")

                def emit_exp_mask(b):
                    bk = batches[b]
                    par = b % 2
                    po0 = par*3*512
                    p.wait("scalar", "s_pe", sc_done[b])
                    if b >= 2:
                        p.wait("scalar", "s_pe", av_done.get(b-2, 0))
                    p.run("scalar", lambda e, po0=po0, par=par, n=len(bk): e.activation(
                        out=ptb[:, po0:po0 + n*512], in_=ps[:, par*3*512: par*3*512 + n*512],
                        func=AF.Exp, scale=SC1), inc="s_act")
                    exp_done[b] = p.cnt("s_act")
                    any_mask = False
                    for j, (ck, jk) in enumerate(bk):
                        if (ck, jk) in kts_part:
                            m = jk - 4*ch
                            if not any_mask:
                                p.wait("vector", "s_act", exp_done[b])
                                any_mask = True
                            for qq in range(m):
                                p.run("vector", lambda e, po0=po0, j=j, qq=qq: e.memset(
                                    ptb[:, po0 + j*512 + qq*128: po0 + j*512 + (qq+1)*128],
                                    0.0), inc="s_dve")
                            if m < 4:
                                p.run("vector", lambda e, po0=po0, j=j, m=m, ck=ck: e.tensor_tensor(
                                    out=ptb[:, po0 + j*512 + m*128: po0 + j*512 + (m+1)*128],
                                    in0=ptb[:, po0 + j*512 + m*128: po0 + j*512 + (m+1)*128],
                                    in1=tri[:, ck*P:(ck+1)*P], op=ALU.mult),
                                    inc="s_dve")
                    msk_done[b] = (p.cnt("s_dve"), any_mask)

                def emit_av(b):
                    bk = batches[b]
                    par = b % 2
                    po0 = par*3*512
                    cnt, any_mask = msk_done[b]
                    if any_mask:
                        p.wait("tensor", "s_dve", cnt)
                    else:
                        p.wait("tensor", "s_act", exp_done[b])
                    gi0 = sum(len(batches[x]) for x in range(b))
                    for j, (ck, jk) in enumerate(bk):
                        kt = ck*8 + jk
                        po = po0 + j*512
                        gi = gi0 + j
                        p.run("tensor", lambda e, kt=kt, po=po, slot=slot, st_=(gi == 0), last=(gi == nkts-1): e.matmul(
                            ps[0:P, 6*512:7*512],
                            vbuf[:, slot*64*P + kt*P: slot*64*P + (kt+1)*P],
                            ptb[:, po:po+512],
                            start=st_, stop=last), inc=None)
                        r_ = gi % nsub
                        cp = 32 * r_
                        tp_kw = {"tile_position": (0, cp)} if cp else {}
                        p.run("tensor", lambda e, po=po, cp=cp, st_=(gi == sub_first[r_]), last=(gi == sub_last[r_]), tp_kw=tp_kw: e.matmul(
                            ps[cp:cp+1, 7*512:7*512+512],
                            ones_col[:, 0:1], ptb[:, po:po+512],
                            start=st_, stop=last, **tp_kw),
                            inc="s_pe" if (j == len(bk)-1) else None)
                    av_done[b] = p.cnt("s_pe")

                for b in range(len(batches)):
                    emit_scores(b)
                    emit_exp_mask(b)
                    if b == 0:
                        p.wait("tensor", "s_dve", norm_done_dve)
                    if b >= 1:
                        emit_av(b-1)
                emit_av(len(batches)-1)
                kv_used_pe[h] = p.cnt("s_pe")
                p.wait("vector", "s_pe", av_done[len(batches)-1])
                p.selfsync("vector")
                p.run("vector", lambda e: e.reciprocal(
                    rec[0:1, 0:512], ps[0:1, 7*512:7*512+512]), inc="s_dve")
                p.run("vector", lambda e: e.tensor_copy(
                    out=tmp[:, 0:512], in_=ps[:, 6*512:7*512]), inc="s_dve")
                rec_done = p.cnt("s_dve")
                p.wait("tensor", "s_dve", rec_done)
                p.run("tensor", lambda e: e.matmul(
                    ps[:, 6*512:7*512], ones_row[0:1, :], rec[0:1, 0:512],
                    start=True, stop=True), inc="s_pe")
                bc_done = p.cnt("s_pe")
                p.wait("vector", "s_pe", bc_done)
                p.run("vector", lambda e, asl=asl: e.tensor_tensor(
                    out=attnT[:, asl], in0=tmp[:, 0:512],
                    in1=ps[:, 6*512:7*512], op=ALU.mult), inc="s_dve")
                norm_done_dve = p.cnt("s_dve")
        att_done_dve = norm_done_dve
        att_done_pe = p.cnt("s_pe")
        if upto <= 1:
            for sname in ("s_st0", "s_st1"):
                p.wait("gpsimd", sname, p.cnt(sname))
            p.wait("gpsimd", "s_dve", att_done_dve)
            p.emit(st)
            return nc

        r3_tiles = [r3[:, t*T:(t+1)*T] for t in range(DT)]
        p.wait("sync", "s_pe", att_done_pe)
        for E_ in range(DT):
            p.run("sync", lambda e, E_=E_: e.dma_start(
                out=r3_tiles[E_], in_=r2T[E_*P:(E_+1)*P, :]), inc="s_ld", dma=True)
        r2_ld = p.cnt("s_ld")
        p.wait("tensor", "s_ld", r2_ld)
        p.wait("vector", "s_ld", r2_ld)
        p.wait("gpsimd", "s_dve", att_done_dve)
        evac_dve = {}
        for E_ in range(DT):
            slot = E_ % 2
            wsem = "s_w%d" % slot
            if E_ >= 2:
                p.wait("gpsimd", "s_dve", evac_dve[E_ - 2])
            p.run("gpsimd", lambda e, E_=E_, slot=slot: e.dma_start(
                out=wstrip[:, slot*DT*P:(slot+1)*DT*P], in_=wout_s[E_]),
                inc=wsem, dma=True)
            p.wait("tensor", wsem, p.cnt(wsem))
            if E_ == 0:
                p.wait("tensor", "s_dve", att_done_dve)
            if E_ >= 2:
                p.wait("tensor", "s_dve", evac_dve[E_ - 2])
            for ch in range(2):
                bank = 4 + (E_ % 2)*2 + ch
                for t in range(DT):
                    p.run("tensor", lambda e, slot=slot, t=t, ch=ch, bank=bank: e.matmul(
                        ps[:, bank*512:(bank+1)*512],
                        wstrip[:, slot*DT*P + t*P: slot*DT*P + (t+1)*P],
                        attnT[:, t*T + ch*512: t*T + (ch+1)*512],
                        start=(t == 0), stop=(t == DT-1)),
                        inc="s_pe" if t == DT-1 else None)
            p.wait("vector", "s_pe")
            p.selfsync("vector")
            for ch in range(2):
                bank = 4 + (E_ % 2)*2 + ch
                sl = slice(E_*T + ch*512, E_*T + (ch+1)*512)
                p.run("vector", lambda e, sl=sl, bank=bank: e.tensor_tensor(
                    out=r3[:, sl], in0=ps[:, bank*512:(bank+1)*512],
                    in1=r3[:, sl], op=ALU.add), inc="s_dve")
            evac_dve[E_] = p.cnt("s_dve")
            p.wait("sync", "s_dve", evac_dve[E_])
            p.run("sync", lambda e, E_=E_: e.dma_start(
                out=r3_d[E_*P:(E_+1)*P, :], in_=r3[:, E_*T:(E_+1)*T]),
                inc="s_st0", dma=True)
        wo_done_dve = p.cnt("s_dve")
        r3_stored = p.cnt("s_st0")
        if upto <= 2:
            for sname in ("s_st0", "s_st1"):
                p.wait("gpsimd", sname, p.cnt(sname))
            p.emit(st)
            return nc

        p.wait("vector", "s_pe", p.cnt("s_pe"))
        p.wait("tensor", "s_dve", wo_done_dve)
        emit_ln(p, r3_tiles, xn3, xbf3, ps, rows, ones_col, ones_row)
        xn3_done = p.cnt("s_dve")
        p.wait("tensor", "s_dve", xn3_done)

        p.wait("vector", "s_st0", r3_stored)
        p.wait("gpsimd", "s_dve", xn3_done)
        ug_stt = {}
        ug_silu = {}
        wug_used = {}
        for E_ in range(HT1):
            for kind in range(2):
                si = E_*2 + kind
                slot = si % 4
                wsem = "s_w%d" % slot
                if si >= 4:
                    p.wait("gpsimd", "s_pe", wug_used[si - 4])
                src = wqg_s if kind == 0 else wqu_s
                p.run("gpsimd", lambda e, src=src, E_=E_, slot=slot: e.dma_start(
                    out=wug4[:, slot*DT*P:(slot+1)*DT*P], in_=src[E_]),
                    inc=wsem, dma=True)
                p.wait("vector", wsem, p.cnt(wsem))
                p.run("vector", lambda e, slot=slot: e.tensor_scalar(
                    out=wug4[:, slot*DT*P:(slot+1)*DT*P],
                    in0=wug4[:, slot*DT*P:(slot+1)*DT*P],
                    scalar1=-128.0, scalar2=None, op0=ALU.add), inc="s_cast")
                p.wait("tensor", "s_cast", p.cnt("s_cast"))
            for ch in range(2):
                gg = E_*2 + ch
                g = gg % 4
                if gg >= 4:
                    p.wait("tensor", "s_dve", ug_stt[gg - 4])
                    p.wait("tensor", "s_act", ug_silu[gg - 4])
                for kind in range(2):
                    slot = (E_*2 + kind) % 4
                    bank = g*2 + kind
                    for t in range(DT):
                        p.run("tensor", lambda e, slot=slot, t=t, ch=ch, bank=bank: e.matmul(
                            ps[:, bank*512:(bank+1)*512],
                            wug4[:, slot*DT*P + t*P: slot*DT*P + (t+1)*P],
                            xn3[:, t*T + ch*512: t*T + (ch+1)*512],
                            start=(t == 0), stop=(t == DT-1)),
                            inc="s_pe" if t == DT-1 else None)
                wug_used[E_*2 + 1] = wug_used[E_*2] = p.cnt("s_pe")
                p.wait("scalar", "s_pe", p.cnt("s_pe"))
                sslot = gg % 4
                p.run("scalar", lambda e, g=g, E_=E_, sslot=sslot: e.activation(
                    out=silu[:, sslot*512:(sslot+1)*512],
                    in_=ps[:, (g*2)*512:(g*2+1)*512],
                    func=AF.Silu, scale=sg[:, E_:E_+1]), inc="s_act")
                ug_silu[gg] = p.cnt("s_act")
                p.wait("vector", "s_act", ug_silu[gg])
                hsl = slice(E_*T + ch*512, E_*T + (ch+1)*512)
                p.run("vector", lambda e, g=g, E_=E_, hsl=hsl, sslot=sslot: e.scalar_tensor_tensor(
                    out=gu[:, hsl], in0=ps[:, (g*2+1)*512:(g*2+2)*512],
                    scalar=su2[:, E_:E_+1], in1=silu[:, sslot*512:(sslot+1)*512],
                    op0=ALU.mult, op1=ALU.mult), inc="s_dve")
                ug_stt[gg] = p.cnt("s_dve")
        ug_done = p.cnt("s_dve")
        if upto <= 4:
            for sname in ("s_st0", "s_st1"):
                p.wait("gpsimd", sname, p.cnt(sname))
            p.wait("gpsimd", "s_dve", ug_done)
            p.emit(st)
            return nc

        p.wait("gpsimd", "s_dve", ug_done)
        p.wait("sync", "s_dve", ug_done)
        dn_stt = {}
        dbuf_used = {}
        for E_ in range(DT):
            slot = E_ % 2
            wsem = "s_w%d" % slot
            if E_ >= 2:
                p.wait("gpsimd", "s_pe", dbuf_used[E_ - 2])
            p.run("gpsimd", lambda e, E_=E_, slot=slot: e.dma_start(
                out=dbuf[:, slot*HT1*P:(slot+1)*HT1*P], in_=wqd_s[E_]),
                inc=wsem, dma=True)
            p.wait("vector", wsem, p.cnt(wsem))
            p.run("vector", lambda e, slot=slot: e.tensor_scalar(
                out=dbuf[:, slot*HT1*P:(slot+1)*HT1*P],
                in0=dbuf[:, slot*HT1*P:(slot+1)*HT1*P],
                scalar1=-128.0, scalar2=None, op0=ALU.add), inc="s_cast")
            p.wait("tensor", "s_cast", p.cnt("s_cast"))
            if E_ >= 2:
                p.wait("tensor", "s_dve", dn_stt[E_ - 2])
            for ch in range(2):
                bank = 4 + (E_ % 2)*2 + (ch % 2)
                rdsem = "s_rd%d" % (ch % 2)
                p.wait("sync", "s_dve", dn_stt.get(E_ - 1, 0))
                p.run("sync", lambda e, E_=E_, ch=ch: e.dma_start(
                    out=rch[:, (ch % 2)*512:(ch % 2 + 1)*512],
                    in_=r3_d[E_*P:(E_+1)*P, ch*512:(ch+1)*512]),
                    inc=rdsem, dma=True)
                for t in range(HT1):
                    p.run("tensor", lambda e, slot=slot, t=t, ch=ch, bank=bank: e.matmul(
                        ps[:, bank*512:(bank+1)*512],
                        dbuf[:, slot*HT1*P + t*P: slot*HT1*P + (t+1)*P],
                        gu[:, t*T + ch*512: t*T + (ch+1)*512],
                        start=(t == 0), stop=(t == HT1-1)),
                        inc="s_pe" if t == HT1-1 else None)
                dbuf_used[E_] = p.cnt("s_pe")
                p.wait("vector", "s_pe")
                p.wait("vector", rdsem, p.cnt(rdsem))
                stsem = "s_st%d" % (ch % 2)
                p.wait("vector", stsem, p.cnt(stsem))
                p.run("vector", lambda e, E_=E_, ch=ch, bank=bank: e.scalar_tensor_tensor(
                    out=stg[:, (ch % 2)*512:(ch % 2+1)*512],
                    in0=ps[:, bank*512:(bank+1)*512],
                    scalar=sd[:, E_:E_+1], in1=rch[:, (ch % 2)*512:(ch % 2+1)*512],
                    op0=ALU.mult, op1=ALU.add), inc="s_dve")
                dn_stt[E_] = p.cnt("s_dve")
                p.wait("sync", "s_dve", dn_stt[E_])
                p.run("sync", lambda e, E_=E_, ch=ch: e.dma_start(
                    out=yT_o[E_*P:(E_+1)*P, ch*512:(ch+1)*512],
                    in_=stg[:, (ch % 2)*512:(ch % 2+1)*512]),
                    inc=stsem, dma=True)
        for sname in ("s_st0", "s_st1"):
            p.wait("gpsimd", sname, p.cnt(sname))
        p.emit(st)
    return nc


def prep_c_inputs(inputs, b_res):
    import ml_dtypes
    k1_all = np.concatenate([np.asarray(b_res[c]["k1T_o"]) for c in range(NC)],
                            axis=1)
    v1_all = np.concatenate([np.asarray(b_res[c]["v1_o"]) for c in range(NC)],
                            axis=0)
    k1Tp = np.ascontiguousarray(k1_all.reshape(DT, P, S))
    v1p = np.ascontiguousarray(
        v1_all.reshape(64, P, 16, 128).transpose(2, 1, 0, 3).reshape(16, P, 64*128))
    kl = np.arange(P)[:, None]
    ql = np.arange(P)[None, :]
    tri_incl = (kl <= ql).astype(ml_dtypes.bfloat16)
    tri_strict = (kl < ql).astype(ml_dtypes.bfloat16)
    wqg = np.asarray(inputs["L1_ffn_wq_gate"])
    wqu = np.asarray(inputs["L1_ffn_wq_up"])
    wqd = np.asarray(inputs["L1_ffn_wq_down"])
    shared = {
        "k1Tp": k1Tp,
        "v1p": v1p,
        "wout_s": col_strips(np.asarray(inputs["L1_attn_w_out"]), DT, DT),
        "wqg_s": col_strips(np.ascontiguousarray(wqg.T), DT, HT1),
        "wqu_s": col_strips(np.ascontiguousarray(wqu.T), DT, HT1),
        "wqd_s": col_strips(np.ascontiguousarray(wqd.T), HT1, DT),
        "sg": vec_tiles(inputs["L1_ffn_s_gate"]),
        "su2": vec_tiles(inputs["L1_ffn_s_up"]),
        "sd": vec_tiles(inputs["L1_ffn_s_down"]),
    }
    maps = []
    for c in range(NC):
        m = dict(shared)
        m["r2T"] = np.asarray(b_res[c]["r2_o"])
        m["q1T_i"] = np.asarray(b_res[c]["q1T_o"])
        tri = np.empty((8, P, P), ml_dtypes.bfloat16)
        for ck in range(NC):
            tri[ck] = tri_incl if ck <= c else tri_strict
        m["tri"] = np.ascontiguousarray(tri.transpose(1, 0, 2).reshape(P, 8*P))
        maps.append(m)
    return maps


def hadamard128():
    i = np.arange(P, dtype=np.uint32)
    pc = np.bitwise_count(i[:, None] & i[None, :])
    return (1.0 - 2.0 * (pc % 2)).astype(np.float32)


def prep_b_inputs(inputs, a_res):
    import ml_dtypes
    x = np.asarray(inputs["L0_attn_x"])[0].astype(np.float32, copy=False)
    kT_all = np.concatenate([np.asarray(a_res[c]["kT_o"]) for c in range(NC)],
                            axis=1)
    v_all = np.concatenate([np.asarray(a_res[c]["v_o"]) for c in range(NC)],
                           axis=0)
    kTp = np.ascontiguousarray(kT_all.reshape(DT, P, S))
    v2 = v_all.reshape(64, P, 16, 128).transpose(2, 1, 0, 3)
    vaug = np.ones((16, P, 64, 130), dtype=ml_dtypes.bfloat16)
    vaug[:, :, :, 0:64] = v2[:, :, :, 0:64]
    vaug[:, :, :, 65:129] = v2[:, :, :, 64:128]
    vaug = vaug.reshape(16, P, 64*130)
    wqkv = np.asarray(inputs["L1_attn_w_qkv"])
    shared = {
        "kTp": kTp,
        "vaug": vaug,
        "Wo_s": col_strips(np.asarray(inputs["L0_attn_Wo"]), DT, DT),
        "bo": vec_tiles(inputs["L0_attn_bo"]),
        "bv": vec_tiles(inputs["L0_attn_bv"]),
        "suug": vec_tiles(inputs["L0_ffn_SU_ug"]),
        "sudp": vec_tiles(inputs["L0_ffn_SU_dp"]),
        "wsu": vec_tiles(np.asarray(inputs["L0_ffn_Wscale_ug"])[:8192]),
        "wsg": vec_tiles(np.asarray(inputs["L0_ffn_Wscale_ug"])[8192:]),
        "wsd": vec_tiles(inputs["L0_ffn_Wscale_dp"]),
        "H128": hadamard128().astype(ml_dtypes.bfloat16),
        "id64": np.eye(64, dtype=ml_dtypes.bfloat16),
        "WupT_s": col_strips(np.ascontiguousarray(
            np.asarray(inputs["L0_ffn_W_up"]).T), DT, 64),
        "WgT_s": col_strips(np.ascontiguousarray(
            np.asarray(inputs["L0_ffn_W_gate"]).T), DT, 64),
        "WdT_s": col_strips(np.ascontiguousarray(
            np.asarray(inputs["L0_ffn_W_down"]).T), 64, DT),
        "wq1_s": col_strips(np.ascontiguousarray(wqkv[:, 0:2048]), DT, DT),
        "wk1_s": col_strips(np.ascontiguousarray(wqkv[:, 2048:4096]), DT, DT),
        "wv1_r": row_strips(np.ascontiguousarray(wqkv[:, 4096:6144]), DT),
    }
    maps = []
    for c in range(NC):
        m = dict(shared)
        m["xT"] = np.ascontiguousarray(x[c::NC].T)
        m["qT_i"] = np.asarray(a_res[c]["qT_o"])
        maps.append(m)
    return maps


def prep_a_inputs(inputs):
    x = np.asarray(inputs["L0_attn_x"])[0].astype(np.float32, copy=False)
    shared = {
        "Wq_s": col_strips(np.asarray(inputs["L0_attn_Wq"]), DT, DT),
        "Wk_s": col_strips(np.asarray(inputs["L0_attn_Wk"]), DT, DT),
        "Wv_r": row_strips(inputs["L0_attn_Wv"], DT),
        "bq": vec_tiles(inputs["L0_attn_bq"]),
        "bk": vec_tiles(inputs["L0_attn_bk"]),
    }
    maps = []
    for c in range(NC):
        m = dict(shared)
        m["xT"] = np.ascontiguousarray(x[c::NC].T)
        maps.append(m)
    return maps



_NC_CACHE = {}


def _get_nc(name, builder):
    if name not in _NC_CACHE:
        _NC_CACHE[name] = builder()
    return _NC_CACHE[name]


def kernel(**inputs):
    maps_a = prep_a_inputs(inputs)
    res_a = run_bass_kernel_spmd(_get_nc("A", build_launch_a), maps_a,
                                 core_ids=list(range(NC))).results
    maps_b = prep_b_inputs(inputs, res_a)
    res_b = run_bass_kernel_spmd(_get_nc("B", build_launch_b), maps_b,
                                 core_ids=list(range(NC))).results
    maps_c = prep_c_inputs(inputs, res_b)
    res_c = run_bass_kernel_spmd(_get_nc("C", build_launch_c), maps_c,
                                 core_ids=list(range(NC))).results
    y = np.empty((S, D), np.float32)
    for c in range(NC):
        y[c::NC] = np.asarray(res_c[c]["yT_o"]).T
    return y[None]
